# version 19 (final)
# speedup vs baseline: 1.1070x; 1.0009x over previous
import contextlib
import numpy as np
import ml_dtypes
import concourse.bass as bass
import concourse.mybir as mybir
from concourse.bass_utils import run_bass_kernel_spmd

F32 = mybir.dt.float32
BF16 = mybir.dt.bfloat16
AF = mybir.ActivationFunctionType
ALU = mybir.AluOpType

NCORE = 8
SEM_NAMES = ["dv", "ac", "mm", "pl", "dma_sy", "dma_pl", "cc", "dma_prm",
             "dma_w0", "dma_w1", "dma_w2", "dma_w3",
             "dma_o0", "dma_o1", "dma_o2", "dma_o3",
             "dma_q0", "dma_q1", "dma_lx", "dma_qo", "dma_qo1"]
T = 1024
T2 = 512
D = 4096
KC = 32
EPS = 1e-5
TT = 1025


class Sems:
    def __init__(self, nc, stk, names):
        self.h = {}
        self.n = {}
        for nm in names:
            self.h[nm] = stk.enter_context(nc.semaphore(nm))
            self.n[nm] = 0

    def inc(self, ins, key, by=1):
        ins.then_inc(self.h[key], by)
        self.n[key] += by
        return self.n[key]


class Eng:
    def __init__(self, e, sems, key):
        self.e = e
        self.s = sems
        self.key = key

    def wait(self, key, val=None):
        v = self.s.n[key] if val is None else val
        self.e.wait_ge(self.s.h[key], v)

    def op(self, fname, *a, gate=True, inc=True, **kw):
        if gate:
            self.wait(self.key)
        ins = getattr(self.e, fname)(*a, **kw)
        if inc:
            self.s.inc(ins, self.key, 1)
        return self.s.n[self.key]

    def dma(self, out, in_, key=None):
        ins = self.e.dma_start(out=out, in_=in_)
        k = key or ("dma_" + self.key)
        self.s.inc(ins, k, 16)
        return self.s.n[k]


def colap(t, off, cnt, step=1):
    full = t[:, :]
    return bass.AP(full.tensor, off, [[full.ap[0][0], 128], [step, cnt]])


def colap2(t, off, dims):
    full = t[:, :]
    return bass.AP(full.tensor, off, [[full.ap[0][0], 128]] + dims)


def dramap(t, off, dims):
    full = t.ap() if hasattr(t, "ap") else t[:, :]
    return bass.AP(full.tensor, off, dims)


def _c128(v):
    v = np.asarray(v, np.float32)
    return np.ascontiguousarray(v.reshape(-1, 128).T)


def prep_l0attn(I):
    f32 = lambda a: np.ascontiguousarray(np.asarray(a, np.float32))
    sh = {}
    sh["WinT"] = f32(I["L0_attn_Win"].T)
    prm = np.zeros((128, 300), np.float32)
    prm[:, 0:4] = _c128(I["L0_attn_bin"])
    packs = (("L0_attn_ln1_w", 4), ("L0_attn_ln1_b", 20), ("L0_attn_ln2_w", 36),
             ("L0_attn_ln2_b", 52), ("L0_attn_sc_lin", 68), ("L0_attn_b2", 84),
             ("L0_attn_phz_re", 100), ("L0_attn_phz_im", 116),
             ("L0_attn_lc_re", 132), ("L0_attn_lc_im", 148))
    for key, c0 in packs:
        for l in range(4):
            prm[:, c0 + 4*l: c0 + 4*l + 4] = _c128(I[key][l])
    for l in range(4):
        prm[:, 164 + 16*l: 164 + 16*l + 16] = _c128(I["L0_attn_b1"][l])
    prm[:, 228:232] = _c128(I["L0_attn_lnl_w"])
    prm[:, 232:236] = _c128(I["L0_attn_lnl_b"])
    prm[:, 236:268] = _c128(I["L0_attn_bout"])
    sh["prm_in"] = np.ascontiguousarray(prm)
    sh["W1T"] = f32(np.stack([I["L0_attn_W1"][l].T for l in range(4)]))
    sh["W2T"] = f32(np.stack([I["L0_attn_W2"][l].T for l in range(4)]))
    sh["WoutT"] = f32(I["L0_attn_Wout"].T)
    return sh


def prep_percore(I):
    percore = [dict() for _ in range(NCORE)]
    xf = np.asarray(I["L0_attn_x"], np.float32).reshape(8192, 4096)
    for c in range(NCORE):
        percore[c]["x_in"] = np.ascontiguousarray(xf[1024*c:1024*(c+1)].T)
        q, b = c % 4, c // 4
        sel = np.zeros((4, 9), np.float32)
        for m in range(q):
            sel[q - 1 - m, 4*b + m] = 1.0
        sel[min(q, 3), 8] = 1.0
        percore[c]["selc"] = np.ascontiguousarray(
            np.broadcast_to(sel.reshape(1, 36), (128, 36)))
    return percore


PC = {"binc": 0, "l1w": 4, "l1b": 20, "l2w": 36, "l2b": 52, "scl": 68,
      "b2c": 84, "phr": 100, "phi": 116, "lcre": 132, "lcim": 148,
      "b1c": 164, "lnlw": 228, "lnlb": 232, "boutc": 236}

SM = {"tmp4": 0, "tmp4b": 4, "r": 8, "rho": 12, "rinv": 16, "ure": 20,
      "uim": 24, "w0re": 28, "w0imn": 32, "Sre": 36, "Sim": 40,
      "fre": 44, "fim": 48, "Fg": 52, "gre": 116, "gim": 120,
      "Pre": 124, "Pim": 128, "Pre2": 132, "Pim2": 136,
      "rho16": 140, "ure16": 156, "uim16": 172}

XG = 0
Y32, YH, WRE, VV = 0, 4096, 8192, 12288
URE, UIM, RHOT = 16384, 20484, 24584
FSCR = 16384
AF_COLS = 28684

WB = 0
XBF, XSQ = 16384, 18432
WF1, WF2 = 0, 8192
Y2B, UB, HBT = 16384, 18432, 26624
OBF, HFIN, SQF, WOT = 0, 16384, 18432, 20480
AB_COLS = 30720



class LNMixin:

    debug = False

    def dump(self, name, t, cols=None):
        if not self.debug:
            return
        nc, S = self.nc, self.S
        full = t[:, :]
        nparts = full.ap[0][1]
        ncols = full.ap[1][1]
        dt = nc.dram_tensor(name, [nparts, ncols], F32, kind="ExternalOutput")
        with nc.Block() as blk:
            @blk.gpsimd
            def _(e):
                pl = Eng(e, S, "pl")
                pl.dma(dt[:, :], full, key="dma_pl")
                e.wait_ge(S.h["dma_pl"], S.n["dma_pl"])

    def dump_arena(self, name, off, ncols, bf=False):
        if not self.debug:
            return
        nc, S = self.nc, self.S
        src_t = self.aB if bf else self.aF
        dt = nc.dram_tensor(name, [128, ncols], BF16 if bf else F32, kind="ExternalOutput")
        with nc.Block() as blk:
            @blk.sync
            def _(e):
                sy = Eng(e, S, "sy")
                sy.dma(dt[:, :], colap(src_t, off, ncols), key="dma_sy")
                e.wait_ge(S.h["dma_sy"], S.n["dma_sy"])


    def ln_rows(self, blk, dim, mark_mm):
        S, rows = self.S, self.rows

        @blk.vector
        def _(e):
            v = Eng(e, S, "dv")
            v.wait("mm", mark_mm)
            v.op("tensor_scalar", rows[:, 0:512], self.psS1[0:1, :], 1.0/dim, None, ALU.mult)
            v.op("tensor_scalar", rows[:, 1024:1536], self.psS2[0:1, :], 1.0/dim, EPS,
                 ALU.mult, ALU.add)
            v.op("tensor_mul", rows[:, 1536:2048], rows[:, 0:512], rows[:, 0:512])
            v.op("tensor_sub", rows[:, 1024:1536], rows[:, 1024:1536], rows[:, 1536:2048])
        m_var = S.n["dv"]

        @blk.scalar
        def _(e):
            sc = Eng(e, S, "ac")
            sc.wait("dv", m_var)
            sc.op("activation", rows[:, 1536:2048], rows[:, 1024:1536], AF.Ln)
            sc.op("activation", rows[:, 512:1024], rows[:, 1536:2048], AF.Exp,
                  scale=-0.5)
        m_sq = S.n["ac"]

        @blk.vector
        def _(e):
            v = Eng(e, S, "dv")
            v.wait("ac", m_sq)
        return S.n["dv"]

    def bcast_mr(self, blk, mark_dv, ps_m, ps_r, m_off=0):
        S = self.S

        @blk.tensor
        def _(e):
            t = Eng(e, S, "mm")
            t.wait("dv", mark_dv)
            t.wait("ac", self.S.n["ac"])
            e.matmul(ps_m[:, :], self.onesrf[:, :], self.rows[:, m_off:m_off+512],
                     start=True, stop=True)
            ins = e.matmul(ps_r[:, :], self.onesrf[:, :], self.rows[:, 512:1024],
                           start=True, stop=True)
            S.inc(ins, "mm", 1)
        m_bc = S.n["mm"]

        @blk.vector
        def _(e):
            v = Eng(e, S, "dv")
            v.wait("mm", m_bc)
            v.op("tensor_copy", self.sbM[:, :], ps_m[:, :])
            v.op("tensor_copy", self.sbR[:, :], ps_r[:, :])
            v.op("tensor_copy", self.sbMb[:, :], self.sbM[:, :])
            v.op("tensor_copy", self.sbRb[:, :], self.sbR[:, :])
        return S.n["dv"]


class L0AttnEmitter(LNMixin):
    def __init__(self, nc, stk, io, sems=None, prefix=""):
        self.nc = nc
        self.io = io
        sb = lambda n, sh, dt=F32: stk.enter_context(nc.sbuf_tensor(prefix + n, sh, dt))
        psf = lambda n: stk.enter_context(nc.psum_tensor(prefix + n, [128, 512], F32))
        self.h32 = sb("h32", [128, 4096])
        self.prm = sb("prm", [128, 300])
        self.selS = sb("selS", [128, 36])
        self.small = sb("small", [128, 188])
        self.onesb = sb("onesb", [128, 1], BF16)
        self.onesrf = sb("onesrf", [1, 128])
        self.rows = sb("rows", [1, 2048])
        self.sbM = sb("sbM", [128, 512])
        self.sbR = sb("sbR", [128, 512])
        self.sbMb = sb("sbMb", [128, 512], BF16)
        self.sbRb = sb("sbRb", [128, 512], BF16)
        self.cw = sb("cw", [1, 512])
        self.aF = sb("arenaF", [128, AF_COLS])
        self.aB = sb("arenaB", [128, AB_COLS], BF16)
        self.psA = psf("psA"); self.psB = psf("psB")
        self.psC = psf("psC"); self.psD = psf("psD")
        self.psM = psf("psM"); self.psE = psf("psE")
        self.psS1 = stk.enter_context(nc.psum_tensor(prefix + "psS1", [1, 512], F32))
        self.psS2 = stk.enter_context(nc.psum_tensor(prefix + "psS2", [1, 512], F32))
        self.S = sems if sems is not None else Sems(nc, stk, SEM_NAMES)
        self.wt_base = [0, 0, 0, 0]

    def pcol(self, name, i):
        c = PC[name] + i
        return self.prm[:, c:c+1]

    def sm(self, name, cnt=4):
        return self.small[:, SM[name]:SM[name]+cnt]


    def inner_stats(self, blk, g):
        S, aB, h32 = self.S, self.aB, self.h32

        @blk.vector
        def _(e):
            v = Eng(e, S, "dv")
            for cg in range(4):
                v.op("tensor_copy", colap(aB, HBT + cg*512, 512),
                     colap(h32, cg*1024 + g*512, 512))
        m_hb = S.n["dv"]

        @blk.vector
        def _(e):
            v = Eng(e, S, "dv")
            for cg in range(4):
                v.op("tensor_mul", colap(aB, HBT + 2048 + cg*512, 512),
                     colap(aB, HBT + cg*512, 512), colap(aB, HBT + cg*512, 512))
        m_sqb = S.n["dv"]

        @blk.tensor
        def _(e):
            t = Eng(e, S, "mm")
            t.wait("dv", m_sqb)
            for cg in range(4):
                e.matmul(self.psS1[:, :], self.onesb[:, :], colap(aB, HBT + cg*512, 512),
                         start=(cg == 0), stop=(cg == 3))
                ins = e.matmul(self.psS2[:, :], self.onesb[:, :],
                               colap(aB, HBT + 2048 + cg*512, 512),
                               start=(cg == 0), stop=(cg == 3))
            S.inc(ins, "mm", 1)
        m_st = S.n["mm"]
        m_rstd = self.ln_rows(blk, 512.0, m_st)
        return self.bcast_mr(blk, m_rstd, self.psM, self.psE)

    def loads(self):
        nc, S, io = self.nc, self.S, self.io
        with nc.Block() as blk:
            @blk.sync
            def _(e):
                sy = Eng(e, S, "sy")
                sy.dma(self.prm[:, 0:300], io["prm_in"][:, :], key="dma_prm")
                sy.dma(self.selS[:, :], io["selc"][:, :], key="dma_prm")

            @blk.gpsimd
            def _(e):
                pl = Eng(e, S, "pl")
                src = dramap(io["WinT"], 0, [[512, 128], [512*128, 32], [1, 512]])
                pl.dma(colap2(self.aB, WB, [[512, 32], [1, 512]]), src, key="dma_pl")

            m_prm_now = S.n["dma_prm"]
            phr16 = self.prm[:, PC["phr"]:PC["phr"] + 16]
            phi16 = self.prm[:, PC["phi"]:PC["phi"] + 16]

            @blk.vector
            def _(e):
                v = Eng(e, S, "dv")
                v.op("memset", self.onesb[:, :], 1.0)
                v.op("memset", self.onesrf[:, :], 1.0)
                v.wait("dma_prm", m_prm_now)
                v.op("tensor_mul", self.sm("ure16", 16), phr16, phr16)
                v.op("tensor_mul", self.sm("uim16", 16), phi16, phi16)
                v.op("tensor_add", self.sm("rho16", 16), self.sm("ure16", 16),
                     self.sm("uim16", 16))
            m_r16 = S.n["dv"]

            @blk.scalar
            def _(e):
                sc = Eng(e, S, "ac")
                sc.wait("dv", m_r16)
                sc.op("activation", self.sm("ure16", 16), self.sm("rho16", 16), AF.Sqrt)
                sc.op("activation", self.sm("rho16", 16), self.sm("ure16", 16), AF.Tanh)
            m_rho16 = S.n["ac"]

            @blk.vector
            def _(e):
                v = Eng(e, S, "dv")
                v.wait("ac", m_rho16)
                v.op("reciprocal", self.sm("uim16", 16), self.sm("ure16", 16))
                v.op("tensor_mul", self.sm("ure16", 16), phr16, self.sm("uim16", 16))
                v.op("tensor_mul", self.sm("uim16", 16), phi16, self.sm("uim16", 16))
        self.m_prm = S.n["dma_prm"]
        self.m_w0 = S.n["dma_pl"]

    def inproj(self):
        nc, S, io, aF, aB = self.nc, self.S, self.io, self.aF, self.aB
        with nc.Block() as blk:
            @blk.tensor
            def _(e):
                t = Eng(e, S, "mm")
                t.wait("dma_pl", self.m_w0)
                t.wait("dv", S.n["dv"])
                for k in range(KC):
                    ins = e.matmul(self.psS1[:, :], self.onesb[:, :],
                                   colap(aB, WB + 512*k, 512),
                                   start=(k == 0), stop=(k == KC - 1))
                S.inc(ins, "mm", 1)
            m_cw = S.n["mm"]

            @blk.vector
            def _(e):
                v = Eng(e, S, "dv")
                v.wait("mm", m_cw)
                v.op("tensor_copy", self.cw[:, :], self.psS1[:, :])

        for g in range(2):
            with nc.Block() as blk:
                @blk.sync
                def _(e):
                    sy = Eng(e, S, "sy")
                    sy.dma(colap2(aF, XG, [[512, KC], [1, 512]]),
                           dramap(io["x_in"], g*512, [[T, 128], [128*T, KC], [1, 512]]),
                           key="dma_sy")
                m_xall = S.n["dma_sy"]

                m_mm0 = S.n["mm"]
                m_dv0 = S.n["dv"]

                @blk.tensor
                def _(e):
                    t = Eng(e, S, "mm")
                    psF = [self.psA, self.psB, self.psC, self.psD]
                    for w in range(8):
                        t.wait("dv", m_dv0 + 8*(w + 1))
                        for kk in range(4):
                            k = 4*w + kk
                            xs = colap(aB, XBF + kk*512, 512)
                            sq = colap(aB, XSQ + kk*512, 512)
                            e.matmul(self.psS1[:, :], self.onesb[:, :], xs,
                                     start=(k == 0), stop=(k == KC - 1))
                            e.matmul(self.psS2[:, :], self.onesb[:, :], sq,
                                     start=(k == 0), stop=(k == KC - 1))
                            for f in range(4):
                                ins = e.matmul(psF[f][:, :],
                                               colap2(aB, WB + 512*k + 128*f, [[1, 128]]),
                                               xs, start=(k == 0), stop=False)
                        S.inc(ins, "mm", 1)

                @blk.vector
                def _(e):
                    v = Eng(e, S, "dv")
                    v.wait("dma_sy", m_xall)
                    for w in range(8):
                        if w >= 1:
                            v.wait("mm", m_mm0 + w)
                        for kk in range(4):
                            k = 4*w + kk
                            v.op("tensor_copy", colap(aB, XBF + kk*512, 512),
                                 colap(aF, XG + 512*k, 512))
                            v.op("tensor_mul", colap(aB, XSQ + kk*512, 512),
                                 colap(aB, XBF + kk*512, 512),
                                 colap(aB, XBF + kk*512, 512))

                m_stats = S.n["mm"]
                m_rstd = self.ln_rows(blk, 4096.0, m_stats)

                @blk.vector
                def _(e):
                    v = Eng(e, S, "dv")
                    v.wait("ac", S.n["ac"])
                    v.op("tensor_scalar", self.rows[:, 1024:1536], self.rows[:, 0:512],
                         -1.0, None, ALU.mult)
                m_negm = S.n["dv"]

                @blk.tensor
                def _(e):
                    t = Eng(e, S, "mm")
                    t.wait("dv", m_negm)
                    t.wait("ac", S.n["ac"])
                    psF = [self.psA, self.psB, self.psC, self.psD]
                    for f in range(4):
                        e.matmul(psF[f][:, :],
                                 self.cw[0:1, 128*f:128*f+128],
                                 self.rows[:, 1024:1536], start=False, stop=True)
                    ins = e.matmul(self.psM[:, :], self.onesrf[:, :],
                                   self.rows[:, 512:1024], start=True, stop=True)
                    S.inc(ins, "mm", 1)
                m_hps = S.n["mm"]

                @blk.vector
                def _(e):
                    v = Eng(e, S, "dv")
                    v.wait("mm", m_hps)
                    v.op("tensor_copy", self.sbR[:, :], self.psM[:, :])
                    psF = [self.psA, self.psB, self.psC, self.psD]
                    for f in range(4):
                        v.op("tensor_mul", colap(aF, YH + f*512, 512),
                             psF[f][:, :], self.sbR[:, :])
                m_hmul = S.n["dv"]

                @blk.scalar
                def _(e):
                    sc = Eng(e, S, "ac")
                    sc.wait("dv", m_hmul)
                    sc.wait("dma_prm", self.m_prm)
                    for f in range(4):
                        sc.op("activation", colap(self.h32, f*1024 + g*512, 512),
                              colap(aF, YH + f*512, 512),
                              AF.Identity, bias=self.pcol("binc", f), scale=1.0)

    def prefetch_ffn(self, l):
        nc, S, io, aB = self.nc, self.S, self.io, self.aB
        with nc.Block() as blk:
            @blk.gpsimd
            def _(e):
                pl = Eng(e, S, "pl")
                src = dramap(io["W1T"], l*512*2048,
                             [[2048, 128], [2048*128, 4], [1, 2048]])
                pl.dma(colap2(aB, WF1, [[2048, 4], [1, 2048]]), src, key="dma_pl")
                src = dramap(io["W2T"], l*2048*512,
                             [[512, 128], [512*128, 16], [1, 512]])
                pl.dma(colap2(aB, WF2, [[512, 16], [1, 512]]), src, key="dma_pl")
            @blk.tensor
            def _(e):
                t = Eng(e, S, "mm")
                t.wait("dma_pl", S.n["dma_pl"])

    def inner_layer(self, l):
        nc, S, io, aF, aB, h32 = self.nc, self.S, self.io, self.aF, self.aB, self.h32
        lw = lambda n, cg: self.pcol(n, 4*l + cg)

        for g in range(2):
            self.inner_ln_apply_l(g, l, "l1w", "l1b", Y32, False, AF.Silu)

        if l == 0:
            self.dump_arena("d_y0", Y32, 4096)
        self.sconv(l)
        if l == 0:
            self.dump_arena("d_conv0", Y32, 4096)

        with nc.Block() as blk:
            @blk.vector
            def _(e):
                v = Eng(e, S, "dv")
                for cg in range(4):
                    v.op("scalar_tensor_tensor", colap(h32, cg*1024, 1024),
                         colap(aF, Y32 + cg*1024, 1024), lw("scl", cg),
                         colap(h32, cg*1024, 1024), ALU.mult, ALU.add)

        for g in range(2):
            self.inner_ln_apply_l(g, l, "l2w", "l2b", Y2B, True, AF.Identity)
            with nc.Block() as blk:
                m_y2 = S.n["ac"]
                m_mm0 = S.n["mm"]

                @blk.tensor
                def _(e):
                    t = Eng(e, S, "mm")
                    for f in range(16):
                        pp = self.psA if (f % 2 == 0) else self.psB
                        if f >= 2:
                            t.wait("ac", m_y2 + (f - 1))
                        for cg in range(4):
                            ins = e.matmul(pp[:, :],
                                           colap2(aB, WF1 + cg*2048 + f*128, [[1, 128]]),
                                           colap(aB, Y2B + cg*512, 512),
                                           start=(cg == 0), stop=(cg == 3))
                        S.inc(ins, "mm", 1)

                @blk.scalar
                def _(e):
                    sc = Eng(e, S, "ac")
                    for f in range(16):
                        sc.wait("mm", m_mm0 + f + 1)
                        pp = self.psA if (f % 2 == 0) else self.psB
                        sc.op("activation", colap(aB, UB + f*512, 512), pp[:, :],
                              AF.Silu, bias=self.pcol("b1c", 16*l + f))
                m_uev0 = S.n["ac"] - 16

                m_mmd0 = S.n["mm"]

                @blk.tensor
                def _(e):
                    t = Eng(e, S, "mm")
                    psF = [self.psC, self.psD, self.psM, self.psE]
                    for cg in range(4):
                        for f in range(16):
                            if cg == 0:
                                t.wait("ac", m_uev0 + f + 1)
                            ins = e.matmul(psF[cg][:, :],
                                           colap2(aB, WF2 + f*512 + cg*128, [[1, 128]]),
                                           colap(aB, UB + f*512, 512),
                                           start=(f == 0), stop=(f == 15))
                        S.inc(ins, "mm", 1)

                @blk.vector
                def _(e):
                    v = Eng(e, S, "dv")
                    psF = [self.psC, self.psD, self.psM, self.psE]
                    if l < 3:
                        self.sconv_tables(v, l + 1, (0, 1) if g == 0 else (2, 3))
                    for cg in range(4):
                        v.wait("mm", m_mmd0 + cg + 1)
                        v.op("scalar_tensor_tensor", colap(h32, cg*1024 + g*512, 512),
                             psF[cg][:, :], lw("b2c", cg),
                             colap(h32, cg*1024 + g*512, 512), ALU.add, ALU.add)

        if l < 3:
            self.prefetch_ffn(l + 1)

    def inner_ln_apply_l(self, g, l, wname, bname, dst_off, dst_bf, act, scr=WRE):
        sv = self.pcol
        nc, S, aF, aB, h32 = self.nc, self.S, self.aF, self.aB, self.h32
        with nc.Block() as blk:
            m_bc = self.inner_stats(blk, g)

            @blk.vector
            def _(e):
                v = Eng(e, S, "dv")
                v.wait("dv", m_bc)
                for cg in range(4):
                    v.op("tensor_sub", colap(aF, scr + cg*512, 512),
                         colap(h32, cg*1024 + g*512, 512), self.sbM[:, :])
                    v.op("tensor_mul", colap(aF, scr + cg*512, 512),
                         colap(aF, scr + cg*512, 512), self.sbR[:, :])
            m_ap = S.n["dv"]

            @blk.scalar
            def _(e):
                sc = Eng(e, S, "ac")
                sc.wait("dv", m_ap)
                for cg in range(4):
                    if dst_bf:
                        dst = colap(aB, dst_off + cg*512, 512)
                    else:
                        dst = colap(aF, dst_off + cg*1024 + g*512, 512)
                    sc.op("activation", dst, colap(aF, scr + cg*512, 512), act,
                          scale=sv(wname, 4*l + cg), bias=sv(bname, 4*l + cg))

    def sconv_tables(self, v, l, cgs):
        aF = self.aF
        for cg in cgs:
            b0 = cg*TT
            c1 = lambda base: self.small[:, SM[base] + 4*l + cg: SM[base] + 4*l + cg + 1]
            v.op("memset", colap(aF, URE + b0, 1), 1.0, gate=(cg == cgs[0]))
            v.op("memset", colap(aF, UIM + b0, 1), 0.0, gate=False)
            v.op("memset", colap(aF, RHOT + b0, 1), 1.0, gate=False)
            v.op("tensor_copy", colap(aF, URE + b0 + 1, 1), c1("ure16"), gate=False)
            v.op("tensor_copy", colap(aF, UIM + b0 + 1, 1), c1("uim16"), gate=False)
            v.op("tensor_copy", colap(aF, RHOT + b0 + 1, 1), c1("rho16"), gate=False)
        n = 2
        while n < TT:
            m = min(n - 1, TT - n)
            for cg in cgs:
                b0 = cg*TT
                v.op("tensor_tensor", colap(aF, YH + cg*m, m), colap(aF, URE + b0 + 1, m),
                     colap(aF, URE + b0 + n - 1, m, 0), ALU.mult, gate=(cg == cgs[0]))
                v.op("tensor_tensor", colap(aF, WRE + cg*m, m), colap(aF, UIM + b0 + 1, m),
                     colap(aF, UIM + b0 + n - 1, m, 0), ALU.mult, gate=False)
                v.op("tensor_tensor", colap(aF, VV + 2*cg*m, m), colap(aF, URE + b0 + 1, m),
                     colap(aF, UIM + b0 + n - 1, m, 0), ALU.mult, gate=False)
                v.op("tensor_tensor", colap(aF, VV + (2*cg+1)*m, m), colap(aF, UIM + b0 + 1, m),
                     colap(aF, URE + b0 + n - 1, m, 0), ALU.mult, gate=False)
                v.op("tensor_tensor", colap(aF, RHOT + b0 + n, m),
                     colap(aF, RHOT + b0 + 1, m),
                     colap(aF, RHOT + b0 + n - 1, m, 0), ALU.mult, gate=False)
            for cg in cgs:
                b0 = cg*TT
                v.op("tensor_tensor", colap(aF, URE + b0 + n, m), colap(aF, YH + cg*m, m),
                     colap(aF, WRE + cg*m, m), ALU.subtract, gate=(cg == cgs[0]))
                v.op("tensor_tensor", colap(aF, UIM + b0 + n, m), colap(aF, VV + 2*cg*m, m),
                     colap(aF, VV + (2*cg+1)*m, m), ALU.add, gate=False)
            n += m

    def sconv(self, l):
        nc, S, io, aF = self.nc, self.S, self.io, self.aF
        sm = self.sm
        T_ = 1024
        u_re = self.small[:, SM["ure16"] + 4*l: SM["ure16"] + 4*l + 4]
        u_im = self.small[:, SM["uim16"] + 4*l: SM["uim16"] + 4*l + 4]

        with nc.Block() as blk:
            @blk.vector
            def _(e):
                v = Eng(e, S, "dv")
                if l == 0:
                    self.sconv_tables(v, 0, (0, 1, 2, 3))
                for cg in range(4):
                    bU, bY = cg*TT, cg*T_
                    v.op("tensor_tensor", colap(aF, YH + bY, T_), colap(aF, Y32 + bY, T_),
                         colap(aF, URE + bU, T_), ALU.mult, gate=(cg == 0))
                for cg in range(4):
                    bY = cg*T_
                    v.op("tensor_tensor_scan", colap(aF, WRE + bY, T_),
                         colap2(self.small, SM["rho16"] + 4*l + cg, [[0, T_]]),
                         colap(aF, YH + bY, T_), 0.0, ALU.mult, ALU.add, gate=(cg == 0))
                for cg in range(4):
                    bU, bY = cg*TT, cg*T_
                    v.op("tensor_tensor", colap(aF, YH + bY, T_), colap(aF, Y32 + bY, T_),
                         colap(aF, UIM + bU, T_), ALU.mult, gate=(cg == 0))
                for cg in range(4):
                    bY = cg*T_
                    v.op("tensor_tensor_scan", colap(aF, VV + bY, T_),
                         colap2(self.small, SM["rho16"] + 4*l + cg, [[0, T_]]),
                         colap(aF, YH + bY, T_), 0.0, ALU.mult, ALU.add, gate=(cg == 0))
                for cg in range(4):
                    bU, bY = cg*TT, cg*T_
                    v.op("tensor_tensor", sm("tmp4")[:, cg:cg+1],
                         colap(aF, URE + bU + T_ - 1, 1),
                         colap(aF, WRE + bY + T_ - 1, 1), ALU.mult, gate=True)
                    v.op("tensor_tensor", sm("tmp4b")[:, cg:cg+1],
                         colap(aF, UIM + bU + T_ - 1, 1),
                         colap(aF, VV + bY + T_ - 1, 1), ALU.mult, gate=False)
                v.op("tensor_add", sm("fre"), sm("tmp4"), sm("tmp4b"))
                for cg in range(4):
                    bU, bY = cg*TT, cg*T_
                    v.op("tensor_tensor", sm("tmp4")[:, cg:cg+1],
                         colap(aF, UIM + bU + T_ - 1, 1),
                         colap(aF, WRE + bY + T_ - 1, 1), ALU.mult, gate=True)
                    v.op("tensor_tensor", sm("tmp4b")[:, cg:cg+1],
                         colap(aF, URE + bU + T_ - 1, 1),
                         colap(aF, VV + bY + T_ - 1, 1), ALU.mult, gate=False)
                v.op("tensor_sub", sm("fim"), sm("tmp4"), sm("tmp4b"))
            m_F = S.n["dv"]

            @blk.gpsimd
            def _(e):
                pl = Eng(e, S, "pl")
                e.wait_ge(S.h["dv"], m_F)
                pl.dma(dramap(io["fag_in"][l], 0, [[1, 512]]), sm("fre"), key="dma_pl")
                pl.dma(dramap(io["fag_in"][l], 512, [[1, 512]]), sm("fim"), key="dma_pl")
                e.wait_ge(S.h["dma_pl"], S.n["dma_pl"])
                ins = e.collective_compute(
                    "AllGather", ALU.bypass, replica_groups=[list(range(NCORE))],
                    ins=[io["fag_in"][l].ap().opt()], outs=[io["fag_out"][l].ap().opt()])
                S.inc(ins, "cc", 1)
                e.wait_ge(S.h["cc"], S.n["cc"])
                for j in range(8):
                    pl.dma(self.small[:, SM["Fg"] + j*8: SM["Fg"] + j*8 + 4],
                           dramap(io["fag_out"][l], j*1024, [[1, 512]]), key="dma_pl")
                    pl.dma(self.small[:, SM["Fg"] + j*8 + 4: SM["Fg"] + j*8 + 8],
                           dramap(io["fag_out"][l], j*1024 + 512, [[1, 512]]), key="dma_pl")
            m_fg = S.n["dma_pl"]

            @blk.vector
            def _(e):
                v = Eng(e, S, "dv")
                for cg in range(4):
                    bU, bY = cg*TT, cg*T_
                    v.op("tensor_tensor", colap(aF, YH + bY, T_),
                         colap(aF, URE + bU, T_), colap(aF, WRE + bY, T_),
                         ALU.mult, gate=(cg == 0))
                for cg in range(4):
                    bU, bY = cg*TT, cg*T_
                    v.op("tensor_tensor", colap(aF, VV + bY, T_),
                         colap(aF, UIM + bU, T_), colap(aF, VV + bY, T_),
                         ALU.mult, gate=(cg == 0))
                for cg in range(4):
                    bY = cg*T_
                    v.op("tensor_tensor", colap(aF, YH + bY, T_),
                         colap(aF, YH + bY, T_), colap(aF, VV + bY, T_),
                         ALU.add, gate=(cg == 0))
                for cg in range(4):
                    bU, bY = cg*TT, cg*T_
                    v.op("tensor_tensor", colap(aF, Y32 + bY, T_),
                         colap(aF, RHOT + bU + 1, T_), colap(aF, URE + bU, T_),
                         ALU.mult, gate=(cg == 0))
                for cg in range(4):
                    bU, bY = cg*TT, cg*T_
                    v.op("tensor_tensor", colap(aF, WRE + bY, T_),
                         colap(aF, RHOT + bU + 1, T_), colap(aF, UIM + bU, T_),
                         ALU.mult, gate=(cg == 0))

            @blk.vector
            def _(e):
                v = Eng(e, S, "dv")
                v.wait("dma_pl", m_fg)
                v.op("memset", sm("Sre"), 0.0)
                v.op("memset", sm("Sim"), 0.0, gate=False)
                for cg in range(4):
                    b0 = cg*TT
                    v.op("tensor_tensor", sm("Pre")[:, cg:cg+1],
                         colap(aF, RHOT + b0 + 1024, 1),
                         colap(aF, URE + b0 + 1024, 1), ALU.mult, gate=(cg == 0))
                    v.op("tensor_tensor", sm("Pim")[:, cg:cg+1],
                         colap(aF, RHOT + b0 + 1024, 1),
                         colap(aF, UIM + b0 + 1024, 1), ALU.mult, gate=False)
                v.op("memset", sm("Pre2"), 1.0)
                v.op("memset", sm("Pim2"), 0.0, gate=False)
                for k in range(4):
                    v.op("memset", sm("gre"), 0.0)
                    v.op("memset", sm("gim"), 0.0, gate=False)
                    for j in range(9):
                        if j < 8:
                            fr = self.small[:, SM["Fg"] + j*8: SM["Fg"] + j*8 + 4]
                            fi = self.small[:, SM["Fg"] + j*8 + 4: SM["Fg"] + j*8 + 8]
                        else:
                            fr = self.prm[:, PC["lcre"] + 4*l: PC["lcre"] + 4*l + 4]
                            fi = self.prm[:, PC["lcim"] + 4*l: PC["lcim"] + 4*l + 4]
                        sel1 = self.selS[:, 9*k + j: 9*k + j + 1]
                        v.op("scalar_tensor_tensor", sm("gre"), fr, sel1, sm("gre"),
                             ALU.mult, ALU.add)
                        v.op("scalar_tensor_tensor", sm("gim"), fi, sel1, sm("gim"),
                             ALU.mult, ALU.add)
                    v.op("tensor_mul", sm("tmp4"), sm("Pre2"), sm("gre"))
                    v.op("tensor_mul", sm("tmp4b"), sm("Pim2"), sm("gim"))
                    v.op("tensor_sub", sm("tmp4"), sm("tmp4"), sm("tmp4b"))
                    v.op("tensor_add", sm("Sre"), sm("Sre"), sm("tmp4"))
                    v.op("tensor_mul", sm("tmp4"), sm("Pre2"), sm("gim"))
                    v.op("tensor_mul", sm("tmp4b"), sm("Pim2"), sm("gre"))
                    v.op("tensor_add", sm("tmp4"), sm("tmp4"), sm("tmp4b"))
                    v.op("tensor_add", sm("Sim"), sm("Sim"), sm("tmp4"))
                    if k < 3:
                        v.op("tensor_mul", sm("tmp4"), sm("Pre2"), sm("Pre"))
                        v.op("tensor_mul", sm("tmp4b"), sm("Pim2"), sm("Pim"))
                        v.op("tensor_sub", sm("tmp4"), sm("tmp4"), sm("tmp4b"))
                        v.op("tensor_mul", sm("tmp4b"), sm("Pim2"), sm("Pre"))
                        v.op("tensor_copy", sm("Pim2"), sm("tmp4b"))
                        v.op("tensor_mul", sm("tmp4b"), sm("Pre2"), sm("Pim"))
                        v.op("tensor_add", sm("Pim2"), sm("Pim2"), sm("tmp4b"))
                        v.op("tensor_copy", sm("Pre2"), sm("tmp4"))
                v.op("tensor_mul", sm("tmp4"), u_re, sm("Sre"))
                v.op("tensor_mul", sm("tmp4b"), u_im, sm("Sim"))
                v.op("tensor_sub", sm("w0re"), sm("tmp4"), sm("tmp4b"))
                v.op("tensor_mul", sm("tmp4"), u_re, sm("Sim"))
                v.op("tensor_mul", sm("tmp4b"), u_im, sm("Sre"))
                v.op("tensor_add", sm("w0imn"), sm("tmp4"), sm("tmp4b"))
                v.op("tensor_scalar", sm("w0imn"), sm("w0imn"), -1.0, None, ALU.mult)
                for cg in range(4):
                    bY = cg*T_
                    v.op("scalar_tensor_tensor", colap(aF, VV + bY, T_),
                         colap(aF, Y32 + bY, T_), sm("w0re")[:, cg:cg+1],
                         colap(aF, YH + bY, T_), ALU.mult, ALU.add)
                    v.op("scalar_tensor_tensor", colap(aF, Y32 + bY, T_),
                         colap(aF, WRE + bY, T_), sm("w0imn")[:, cg:cg+1],
                         colap(aF, VV + bY, T_), ALU.mult, ALU.add)

    def final(self):
        nc, S, io, aF, aB, h32 = self.nc, self.S, self.io, self.aF, self.aB, self.h32
        for g in range(2):
            self.inner_ln_apply_l(g, 0, "lnlw", "lnlb", HFIN, True, AF.Identity,
                                  scr=FSCR)
            with nc.Block() as blk:
                @blk.sync
                def _(e):
                    sy = Eng(e, S, "sy")
                    if g == 1:
                        e.wait_ge(S.h["dma_sy"], S.n["dma_sy"])
                    sy.dma(colap2(aF, XG, [[512, KC], [1, 512]]),
                           dramap(io["x_in"], g*512, [[T, 128], [128*T, KC], [1, 512]]),
                           key="dma_sy")
                m_xall = S.n["dma_sy"]

                m_mm0 = S.n["mm"]
                m_ac0 = S.n["ac"]

                @blk.gpsimd
                def _(e):
                    pl = Eng(e, S, "pl")
                    for f in range(32):
                        if f >= 4:
                            e.wait_ge(S.h["mm"], m_mm0 + f - 3)
                        src = dramap(io["WoutT"], 128*f,
                                     [[4096, 128], [4096*128, 4], [1, 128]])
                        ins = e.dma_start(
                            out=colap2(aB, WOT + (f % 4)*512, [[128, 4], [1, 128]]),
                            in_=src)
                        S.inc(ins, "dma_w%d" % (f % 4), 16)

                @blk.tensor
                def _(e):
                    t = Eng(e, S, "mm")
                    for f in range(32):
                        pp = self.psA if (f % 2 == 0) else self.psB
                        t.wait("dma_w%d" % (f % 4), 16*(f // 4 + 1) + self.wt_base[f % 4])
                        if f >= 2:
                            t.wait("ac", m_ac0 + (f - 1))
                        for cg in range(4):
                            ins = e.matmul(pp[:, :],
                                           colap2(aB, WOT + (f % 4)*512 + cg*128, [[1, 128]]),
                                           colap(aB, HFIN + cg*512, 512),
                                           start=(cg == 0), stop=(cg == 3))
                        S.inc(ins, "mm", 1)
                for i in range(4):
                    self.wt_base[i] += 16*8

                @blk.scalar
                def _(e):
                    sc = Eng(e, S, "ac")
                    for f in range(32):
                        sc.wait("mm", m_mm0 + f + 1)
                        pp = self.psA if (f % 2 == 0) else self.psB
                        sc.op("activation", colap(aB, OBF + f*512, 512), pp[:, :],
                              AF.Identity, bias=self.pcol("boutc", f))
                m_oev0 = S.n["ac"] - 32

                m_dv0 = S.n["dv"]
                m_mmS0 = S.n["mm"]

                @blk.vector
                def _(e):
                    v = Eng(e, S, "dv")
                    for w in range(8):
                        if w >= 1:
                            v.wait("mm", m_mmS0 + w)
                        for kk in range(4):
                            f = 4*w + kk
                            v.wait("ac", m_oev0 + f + 1)
                            v.op("tensor_mul", colap(aB, SQF + kk*512, 512),
                                 colap(aB, OBF + f*512, 512),
                                 colap(aB, OBF + f*512, 512), gate=False)

                @blk.tensor
                def _(e):
                    t = Eng(e, S, "mm")
                    for w in range(8):
                        t.wait("dv", m_dv0 + 4*(w + 1))
                        for kk in range(4):
                            f = 4*w + kk
                            e.matmul(self.psS1[:, :], self.onesb[:, :],
                                     colap(aB, OBF + f*512, 512),
                                     start=(f == 0), stop=(f == 31))
                            ins = e.matmul(self.psS2[:, :], self.onesb[:, :],
                                           colap(aB, SQF + kk*512, 512),
                                           start=(f == 0), stop=(f == 31))
                        S.inc(ins, "mm", 1)

                m_st = S.n["mm"]
                m_rstd = self.ln_rows(blk, 4096.0, m_st)
                m_bc = self.bcast_mr(blk, m_rstd, self.psC, self.psD)

                @blk.vector
                def _(e):
                    v = Eng(e, S, "dv")
                    v.wait("dv", m_bc)
                    v.wait("dma_sy", m_xall)
                    for k in range(KC):
                        v.op("tensor_sub", colap(aB, SQF + (k % 4)*512, 512),
                             colap(aB, OBF + k*512, 512), self.sbMb[:, :])
                        v.op("tensor_mul", colap(aB, WOT + (k % 4)*512, 512),
                             colap(aB, SQF + (k % 4)*512, 512), self.sbRb[:, :])
                        v.op("tensor_add", colap(aF, XG + 512*k, 512),
                             colap(aF, XG + 512*k, 512),
                             colap(aB, WOT + (k % 4)*512, 512))
                m_xn0 = S.n["dv"] - 3*KC

                @blk.sync
                def _(e):
                    sy = Eng(e, S, "sy")
                    for k in range(KC):
                        e.wait_ge(S.h["dv"], m_xn0 + 3*(k + 1))
                        sy.dma(dramap(io["x_out"], k*128*T + g*512, [[T, 128], [1, 512]]),
                               colap(aF, XG + 512*k, 512), key="dma_sy")
                    if g == 1:
                        e.wait_ge(S.h["dma_sy"], S.n["dma_sy"])


def build_l0attn(debug=False):
    nc = bass.Bass("TRN2", target_bir_lowering=False)
    io = {}
    def din(name, shape, dt=F32):
        io[name] = nc.dram_tensor(name, shape, dt, kind="ExternalInput")
        return io[name]
    din("x_in", [D, T])
    din("WinT", [D, 512])
    din("prm_in", [128, 300])
    din("selc", [128, 36])
    din("W1T", [4, 512, 2048])
    din("W2T", [4, 2048, 512])
    din("WoutT", [512, 4096])
    io["x_out"] = nc.dram_tensor("x_out", [D, T], F32, kind="ExternalOutput")
    io["fag_in"] = [nc.dram_tensor(f"fag_in{l}", [1, 1024], F32) for l in range(4)]
    io["fag_out"] = [nc.dram_tensor(f"fag_out{l}", [8, 1024], F32) for l in range(4)]

    with contextlib.ExitStack() as stk:
        em = L0AttnEmitter(nc, stk, io)
        em.debug = debug
        em.loads()
        em.inproj()
        em.prefetch_ffn(0)
        em.dump("d_h0", em.h32)
        em.dump("d_rows", em.rows)
        em.dump("d_cw", em.cw)
        for l in range(4):
            em.inner_layer(l)
            em.dump(f"d_h{l+1}", em.h32)
        em.final()
    return nc



def prep_l0ffn(I):
    f32 = lambda a: np.ascontiguousarray(np.asarray(a, np.float32))
    sh = {}
    sh["w1qT"] = f32(I["L0_ffn_w1q"].T)
    sh["w3qT"] = f32(I["L0_ffn_w3q"].T)
    sh["w2qT"] = f32(I["L0_ffn_w2q"].T)
    sh["a1T"] = f32(I["L0_ffn_a1"].T)
    sh["a3T"] = f32(I["L0_ffn_a3"].T)
    sh["a2T"] = f32(I["L0_ffn_a2"].T)
    sh["b1T"] = f32(I["L0_ffn_b1"].T)
    sh["b3T"] = f32(I["L0_ffn_b3"].T)
    sh["b2T"] = f32(I["L0_ffn_b2"].T)
    return sh


F_XS = 0
F_GU = 8192
F_W13 = 30720
F_XBF = 47104
F_XSQ = 49152
F_GT = 47104
F_GV = 51200
F_O2 = 30720
F_W2S = 47104
F_DSC1 = 0
F_DSC2 = 2048
FB_COLS = 55296


def _capW(ap, Wc):
    if Wc == 128:
        return ap
    return bass.AP(ap.tensor, ap.offset, [[ap.ap[0][0], Wc]] + [list(d) for d in ap.ap[1:]])


class L0FfnEmitter(LNMixin):
    def __init__(self, nc, stk, io, sems=None, prefix=""):
        self.nc = nc
        self.io = io
        sb = lambda n, sh, dt=F32: stk.enter_context(nc.sbuf_tensor(prefix + n, sh, dt))
        psf = lambda n: stk.enter_context(nc.psum_tensor(prefix + n, [128, 512], F32))
        self.onesb = sb("onesb", [128, 1], BF16)
        self.onesf = sb("onesf", [128, 1])
        self.onesrf = sb("onesrf", [1, 128])
        self.rows = sb("rows", [1, 2048])
        self.sbM = sb("sbM", [128, 512])
        self.sbR = sb("sbR", [128, 512])
        self.sbMb = sb("sbMb", [128, 512], BF16)
        self.sbRb = sb("sbRb", [128, 512], BF16)
        self.a1S = sb("a1S", [128, 64])
        self.a3S = sb("a3S", [128, 64])
        self.a1Sb = sb("a1Sb", [128, 64], BF16)
        self.a3Sb = sb("a3Sb", [128, 64], BF16)
        self.a2sb = sb("a2sb", [128, 16], BF16)
        self.bls = sb("bls", [2, 2048], BF16)
        self.b2t = sb("b2t", [2, 256], BF16)
        self.canc = sb("canc", [2, 2])
        self.xah = sb("xah", [2, 1024], BF16)
        self.xah3 = sb("xah3", [2, 1024], BF16)
        self.guah = sb("guah", [2, 1024], BF16)
        self.xa32 = sb("xa32", [2, 2048])
        self.aF = sb("arenaF", [128, 16384])
        self.aB = sb("arenaB", [128, FB_COLS], BF16)
        self.psA = psf("psA"); self.psB = psf("psB")
        self.psC = psf("psC"); self.psD = psf("psD")
        self.psM = psf("psM"); self.psE = psf("psE")
        self.psS1 = stk.enter_context(nc.psum_tensor(prefix + "psS1", [1, 512], F32))
        self.psS2 = stk.enter_context(nc.psum_tensor(prefix + "psS2", [1, 512], F32))
        self.S = sems if sems is not None else Sems(nc, stk, SEM_NAMES)

    def loads(self):
        nc, S, io = self.nc, self.S, self.io
        with nc.Block() as blk:
            @blk.sync
            def _(e):
                sy = Eng(e, S, "sy")
                for nm, dst in (("a1T", self.a1S), ("a3T", self.a3S)):
                    sy.dma(colap2(dst, 0, [[2, 32], [1, 2]]),
                           dramap(io[nm], 0, [[2, 128], [256, 32], [1, 2]]),
                           key="dma_prm")
            @blk.vector
            def _(e):
                v = Eng(e, S, "dv")
                v.op("memset", self.onesb[:, :], 1.0)
                v.op("memset", self.onesf[:, :], 1.0)
                v.op("memset", self.onesrf[:, :], 1.0)
        self.m_prm = S.n["dma_prm"]
        with nc.Block() as blk:
            @blk.vector
            def _(e):
                v = Eng(e, S, "dv")
                v.wait("dma_prm", self.m_prm)
                v.op("tensor_copy", self.a1Sb[:, :], self.a1S[:, :])
                v.op("tensor_copy", self.a3Sb[:, :], self.a3S[:, :])

    def ln_phase(self):
        nc, S, io, aF, aB = self.nc, self.S, self.io, self.aF, self.aB
        with nc.Block() as blk:
            @blk.tensor
            def _(e):
                t = Eng(e, S, "mm")
                t.wait("dma_prm", self.m_prm)
                t.wait("dv", S.n["dv"])
                for k in range(KC):
                    e.matmul(self.psC[0:2, 0:1], self.a1S[:, 2*k:2*k+2], self.onesf[:, :],
                             start=(k == 0), stop=(k == KC - 1))
                    ins = e.matmul(self.psD[0:2, 0:1], self.a3S[:, 2*k:2*k+2], self.onesf[:, :],
                                   start=(k == 0), stop=(k == KC - 1))
                S.inc(ins, "mm", 1)
            m_ca = S.n["mm"]

            @blk.vector
            def _(e):
                v = Eng(e, S, "dv")
                v.wait("mm", m_ca)
                v.op("tensor_scalar", self.canc[:, 0:1], self.psC[0:2, 0:1], -1.0, None, ALU.mult)
                v.op("tensor_scalar", self.canc[:, 1:2], self.psD[0:2, 0:1], -1.0, None, ALU.mult)

        for g in range(2):
            with nc.Block() as blk:
                @blk.sync
                def _(e):
                    sy = Eng(e, S, "sy")
                    sy.dma(colap2(aF, 0, [[512, KC], [1, 512]]),
                           dramap(io["x_in"], g*512, [[T, 128], [128*T, KC], [1, 512]]),
                           key="dma_sy")
                m_xall = S.n["dma_sy"]

                m_mm0 = S.n["mm"]
                m_dv0 = S.n["dv"]

                @blk.tensor
                def _(e):
                    t = Eng(e, S, "mm")
                    for w in range(8):
                        t.wait("dv", m_dv0 + 8*(w + 1))
                        for kk in range(4):
                            k = 4*w + kk
                            xs_ = colap(aB, F_XBF + kk*512, 512)
                            sq_ = colap(aB, F_XSQ + kk*512, 512)
                            e.matmul(self.psS1[:, :], self.onesb[:, :], xs_,
                                     start=(k == 0), stop=(k == KC - 1))
                            e.matmul(self.psS2[:, :], self.onesb[:, :], sq_,
                                     start=(k == 0), stop=(k == KC - 1))
                            e.matmul(self.psA[0:2, :], self.a1Sb[:, 2*k:2*k+2], xs_,
                                     start=(k == 0), stop=(k == KC - 1))
                            ins = e.matmul(self.psB[0:2, :], self.a3Sb[:, 2*k:2*k+2], xs_,
                                           start=(k == 0), stop=(k == KC - 1))
                        S.inc(ins, "mm", 1)

                @blk.vector
                def _(e):
                    v = Eng(e, S, "dv")
                    v.wait("dma_sy", m_xall)
                    for w in range(8):
                        if w >= 1:
                            v.wait("mm", m_mm0 + w)
                        for kk in range(4):
                            k = 4*w + kk
                            v.op("tensor_copy", colap(aB, F_XBF + kk*512, 512),
                                 colap(aF, 512*k, 512))
                            v.op("tensor_mul", colap(aB, F_XSQ + kk*512, 512),
                                 colap(aB, F_XBF + kk*512, 512),
                                 colap(aB, F_XBF + kk*512, 512))

                m_stats = S.n["mm"]
                m_rstd = self.ln_rows(blk, 4096.0, m_stats)

                @blk.vector
                def _(e):
                    v = Eng(e, S, "dv")
                    v.wait("ac", S.n["ac"])
                    v.op("tensor_scalar", self.rows[:, 1024:1536], self.rows[:, 0:512],
                         4.0, None, ALU.mult)
                m_m4 = S.n["dv"]
                m_bc = self.bcast_mr(blk, m_m4, self.psM, self.psE, m_off=1024)

                @blk.tensor
                def _(e):
                    t = Eng(e, S, "mm")
                    t.wait("dv", m_bc)
                    e.matmul(self.psC[0:2, :], self.onesrf[0:1, 0:2], self.rows[:, 0:512],
                             start=True, stop=True)
                    ins = e.matmul(self.psD[0:2, :], self.onesrf[0:1, 0:2], self.rows[:, 512:1024],
                                   start=True, stop=True)
                    S.inc(ins, "mm", 1)
                m_bc2 = S.n["mm"]

                @blk.vector
                def _(e):
                    v = Eng(e, S, "dv")
                    v.wait("mm", m_bc2)
                    for j in range(8):
                        for kk in (8, 16, 24):
                            v.op("tensor_add", colap(aF, 512*j, 512),
                                 colap(aF, 512*j, 512), colap(aF, 512*(j+kk), 512))
                    for j in range(8):
                        v.op("tensor_sub", colap(aF, 512*(j+8), 512),
                             colap(aF, 512*j, 512), self.sbM[:, :])
                        v.op("tensor_mul", colap(aB, F_XS + j*1024 + g*512, 512),
                             colap(aF, 512*(j+8), 512), self.sbR[:, :])
                    for i, ps in ((0, self.psA), (1, self.psB)):
                        base = i*1024 + g*512
                        v.op("tensor_copy", self.xa32[:, base:base+512], ps[0:2, :])
                        v.op("scalar_tensor_tensor", self.xa32[:, base:base+512],
                             self.psC[0:2, :], self.canc[:, i:i+1],
                             self.xa32[:, base:base+512], ALU.mult, ALU.add)
                        v.op("tensor_tensor", self.xa32[:, base:base+512],
                             self.xa32[:, base:base+512], self.psD[0:2, :], ALU.mult)
                    v.op("tensor_scalar", self.xah[:, g*512:(g+1)*512],
                         self.xa32[:, g*512:g*512+512], 0.5, None, ALU.mult)
                    v.op("tensor_scalar", self.xah3[:, g*512:(g+1)*512],
                         self.xa32[:, 1024+g*512:1024+g*512+512], 0.5, None, ALU.mult)

    def gateup_phase(self):
        nc, S, io, aB = self.nc, self.S, self.io, self.aB
        for c in range(22):
            Wc = 128 if c < 21 else 64
            slot = c % 2
            sb_w = F_W13 + slot*8192
            wkey = "dma_w%d" % slot
            with nc.Block() as blk:
                @blk.gpsimd
                def _(e):
                    pl = Eng(e, S, "pl")
                    off = 128*c
                    pl.dma(colap2(aB, sb_w, [[1024, 4], [128, 8], [1, Wc]]),
                           dramap(io["w1qT"], off,
                                  [[11008, 128], [2752, 4], [128*11008, 8], [1, Wc]]),
                           key=wkey)
                    pl.dma(colap2(aB, sb_w + 4096, [[1024, 4], [128, 8], [1, Wc]]),
                           dramap(io["w3qT"], off,
                                  [[11008, 128], [2752, 4], [128*11008, 8], [1, Wc]]),
                           key=wkey)
                    pl.dma(bass.AP(self.bls[:, :].tensor, slot*1024,
                                   [[self.bls[:, :].ap[0][0], 2], [128, 4], [1, Wc]]),
                           dramap(io["b1T"], off, [[11008, 2], [2752, 4], [1, Wc]]),
                           key=wkey)
                    pl.dma(bass.AP(self.bls[:, :].tensor, slot*1024 + 512,
                                   [[self.bls[:, :].ap[0][0], 2], [128, 4], [1, Wc]]),
                           dramap(io["b3T"], off, [[11008, 2], [2752, 4], [1, Wc]]),
                           key=wkey)
                    pl.dma(bass.AP(self.a2sb[:, :].tensor, slot*8,
                                   [[self.a2sb[:, :].ap[0][0], Wc], [2, 4], [1, 2]]),
                           dramap(io["a2T"], off*2, [[2, Wc], [2752*2, 4], [1, 2]]),
                           key=wkey)
                m_w = S.n[wkey]

                m_ac0 = S.n["ac"]
                m_dv0 = S.n["dv"]
                m_mm0 = S.n["mm"]

                @blk.tensor
                def _(e):
                    t = Eng(e, S, "mm")
                    t.wait(wkey, m_w)
                    if c < 2:
                        t.wait("dv", m_dv0)
                    psQ = [self.psA, self.psB, self.psC, self.psD]
                    for i in range(8):
                        k4, g = i // 2, i % 2
                        pg, pu = psQ[(2*i) % 4], psQ[(2*i + 1) % 4]
                        if i >= 2:
                            t.wait("ac", m_ac0 + (i - 1))
                            t.wait("dv", m_dv0 + (i - 1))
                        for kd in range(8):
                            e.matmul(pg[0:Wc, :],
                                     colap2(aB, sb_w + k4*1024 + kd*128, [[1, Wc]]),
                                     colap(aB, F_XS + kd*1024 + g*512, 512),
                                     start=(kd == 0), stop=False)
                            e.matmul(pu[0:Wc, :],
                                     colap2(aB, sb_w + 4096 + k4*1024 + kd*128, [[1, Wc]]),
                                     colap(aB, F_XS + kd*1024 + g*512, 512),
                                     start=(kd == 0), stop=False)
                        e.matmul(pg[0:Wc, :],
                                 bass.AP(self.bls[:, :].tensor, slot*1024 + k4*128,
                                         [[self.bls[:, :].ap[0][0], 2], [1, Wc]]),
                                 self.xah[:, g*512:(g+1)*512], start=False, stop=True)
                        ins = e.matmul(pu[0:Wc, :],
                                       bass.AP(self.bls[:, :].tensor, slot*1024 + 512 + k4*128,
                                               [[self.bls[:, :].ap[0][0], 2], [1, Wc]]),
                                       self.xah3[:, g*512:(g+1)*512], start=False, stop=True)
                        S.inc(ins, "mm", 1)
                    t.wait("dv", m_dv0 + 8 + 6)
                    for i in range(8):
                        k4, g = i // 2, i % 2
                        psT = self.psE if g == 0 else self.psM
                        ins = e.matmul(psT[0:2, :],
                                       bass.AP(self.a2sb[:, :].tensor, slot*8 + k4*2,
                                               [[self.a2sb[:, :].ap[0][0], Wc], [1, 2]]),
                                       _capW(colap(aB, F_GV + i*512, 512), Wc),
                                       start=(c == 0 and k4 == 0),
                                       stop=(c == 21 and k4 == 3))
                    S.inc(ins, "mm", 1)

                @blk.scalar
                def _(e):
                    sc = Eng(e, S, "ac")
                    for i in range(8):
                        sc.wait("mm", m_mm0 + i + 1)
                        pg = [self.psA, self.psB, self.psC, self.psD][(2*i) % 4]
                        sc.op("activation", _capW(colap(aB, F_GT + i*512, 512), Wc),
                              pg[0:Wc, :], AF.Silu)

                @blk.vector
                def _(e):
                    v = Eng(e, S, "dv")
                    for i in range(8):
                        pu = [self.psA, self.psB, self.psC, self.psD][(2*i + 1) % 4]
                        v.wait("ac", m_ac0 + i + 1)
                        v.op("tensor_tensor", _capW(colap(aB, F_GV + i*512, 512), Wc),
                             _capW(colap(aB, F_GT + i*512, 512), Wc),
                             pu[0:Wc, :], ALU.mult)
                    for g in range(2):
                        dst = _capW(colap(aB, F_GU + c*1024 + g*512, 512), Wc)
                        v.op("tensor_tensor", dst, _capW(colap(aB, F_GV + g*512, 512), Wc),
                             _capW(colap(aB, F_GV + 1024 + g*512, 512), Wc), ALU.add)
                        v.op("tensor_tensor", dst, dst,
                             _capW(colap(aB, F_GV + 2048 + g*512, 512), Wc), ALU.add)
                        v.op("tensor_tensor", dst, dst,
                             _capW(colap(aB, F_GV + 3072 + g*512, 512), Wc), ALU.add)

        with nc.Block() as blk:
            @blk.vector
            def _(e):
                v = Eng(e, S, "dv")
                v.op("tensor_scalar", self.guah[:, 0:512], self.psE[0:2, :], 0.5, None, ALU.mult)
                v.op("tensor_scalar", self.guah[:, 512:1024], self.psM[0:2, :], 0.5, None, ALU.mult)

    def down_phase(self):
        nc, S, io, aB, aF = self.nc, self.S, self.io, self.aB, self.aF
        for g in range(2):
            with nc.Block() as blk:
                @blk.sync
                def _(e):
                    sy = Eng(e, S, "sy")
                    if g == 1:
                        e.wait_ge(S.h["dma_sy"], S.n["dma_sy"])
                    sy.dma(colap2(aF, 0, [[512, KC], [1, 512]]),
                           dramap(io["x_in"], g*512, [[T, 128], [128*T, KC], [1, 512]]),
                           key="dma_sy")
                m_xall = S.n["dma_sy"]

                m_mm0 = S.n["mm"]
                m_ac0 = S.n["ac"]
                wb = [S.n["dma_w2"], S.n["dma_w3"]]

                @blk.gpsimd
                def _(e):
                    pl = Eng(e, S, "pl")
                    for f in range(32):
                        slot = f % 2
                        wkey = "dma_w%d" % (2 + slot)
                        if f >= 2:
                            e.wait_ge(S.h["mm"], m_mm0 + f - 1)
                        pl.dma(colap2(aB, F_W2S + slot*2816, [[128, 21], [1, 128]]),
                               dramap(io["w2qT"], 128*f, [[4096, 128], [128*4096, 21], [1, 128]]),
                               key=wkey)
                        pl.dma(bass.AP(self.aB[:, :].tensor, F_W2S + slot*2816 + 21*128,
                                       [[self.aB[:, :].ap[0][0], 64], [1, 128]]),
                               dramap(io["w2qT"], 2688*4096 + 128*f, [[4096, 64], [1, 128]]),
                               key=wkey)
                        pl.dma(bass.AP(self.b2t[:, :].tensor, slot*128,
                                       [[self.b2t[:, :].ap[0][0], 2], [1, 128]]),
                               dramap(io["b2T"], 128*f, [[4096, 2], [1, 128]]),
                               key=wkey)

                @blk.tensor
                def _(e):
                    t = Eng(e, S, "mm")
                    for f in range(32):
                        slot = f % 2
                        pp = self.psA if (f % 2 == 0) else self.psB
                        t.wait("dma_w%d" % (2 + slot), wb[slot] + 48*(f // 2 + 1))
                        if f >= 2:
                            t.wait("ac", m_ac0 + (f - 1))
                        for cc in range(21):
                            e.matmul(pp[:, :],
                                     colap2(aB, F_W2S + slot*2816 + cc*128, [[1, 128]]),
                                     colap(aB, F_GU + cc*1024 + g*512, 512),
                                     start=(cc == 0), stop=False)
                        e.matmul(pp[:, :],
                                 bass.AP(self.aB[:, :].tensor, F_W2S + slot*2816 + 21*128,
                                         [[self.aB[:, :].ap[0][0], 64], [1, 128]]),
                                 _capW(colap(aB, F_GU + 21*1024 + g*512, 512), 64),
                                 start=False, stop=False)
                        ins = e.matmul(pp[:, :],
                                       bass.AP(self.b2t[:, :].tensor, slot*128,
                                               [[self.b2t[:, :].ap[0][0], 2], [1, 128]]),
                                       self.guah[:, g*512:(g+1)*512],
                                       start=False, stop=True)
                        S.inc(ins, "mm", 1)

                @blk.scalar
                def _(e):
                    sc = Eng(e, S, "ac")
                    for f in range(32):
                        sc.wait("mm", m_mm0 + f + 1)
                        pp = self.psA if (f % 2 == 0) else self.psB
                        sc.op("activation", colap(aB, F_O2 + f*512, 512), pp[:, :],
                              AF.Identity)
                m_oev0 = S.n["ac"] - 32

                m_dv0 = S.n["dv"]
                m_mmS0 = S.n["mm"]

                @blk.vector
                def _(e):
                    v = Eng(e, S, "dv")
                    for w in range(8):
                        if w >= 1:
                            v.wait("mm", m_mmS0 + w)
                        for kk in range(4):
                            f = 4*w + kk
                            v.wait("ac", m_oev0 + f + 1)
                            v.op("tensor_mul", colap(aB, F_DSC1 + kk*512, 512),
                                 colap(aB, F_O2 + f*512, 512),
                                 colap(aB, F_O2 + f*512, 512), gate=False)

                @blk.tensor
                def _(e):
                    t = Eng(e, S, "mm")
                    for w in range(8):
                        t.wait("dv", m_dv0 + 4*(w + 1))
                        for kk in range(4):
                            f = 4*w + kk
                            e.matmul(self.psS1[:, :], self.onesb[:, :],
                                     colap(aB, F_O2 + f*512, 512),
                                     start=(f == 0), stop=(f == 31))
                            ins = e.matmul(self.psS2[:, :], self.onesb[:, :],
                                           colap(aB, F_DSC1 + kk*512, 512),
                                           start=(f == 0), stop=(f == 31))
                        S.inc(ins, "mm", 1)

                m_st = S.n["mm"]
                m_rstd = self.ln_rows(blk, 4096.0, m_st)
                m_bc = self.bcast_mr(blk, m_rstd, self.psC, self.psD)

                @blk.vector
                def _(e):
                    v = Eng(e, S, "dv")
                    v.wait("dv", m_bc)
                    v.wait("dma_sy", m_xall)
                    for k in range(KC):
                        v.op("tensor_sub", colap(aB, F_DSC1 + (k % 4)*512, 512),
                             colap(aB, F_O2 + k*512, 512), self.sbMb[:, :])
                        v.op("tensor_mul", colap(aB, F_DSC2 + (k % 4)*512, 512),
                             colap(aB, F_DSC1 + (k % 4)*512, 512), self.sbRb[:, :])
                        v.op("tensor_add", colap(aF, 512*k, 512),
                             colap(aF, 512*k, 512), colap(aB, F_DSC2 + (k % 4)*512, 512))
                m_xn0 = S.n["dv"] - 3*KC

                @blk.sync
                def _(e):
                    sy = Eng(e, S, "sy")
                    for k in range(KC):
                        e.wait_ge(S.h["dv"], m_xn0 + 3*(k + 1))
                        sy.dma(dramap(io["x_out"], k*128*T + g*512, [[T, 128], [1, 512]]),
                               colap(aF, 512*k, 512), key="dma_sy")
                    if g == 1:
                        e.wait_ge(S.h["dma_sy"], S.n["dma_sy"])


def build_l0ffn(debug=False):
    nc = bass.Bass("TRN2", target_bir_lowering=False)
    io = {}
    def din(name, shape, dt=F32):
        io[name] = nc.dram_tensor(name, shape, dt, kind="ExternalInput")
        return io[name]
    din("x_in", [D, T])
    din("w1qT", [1024, 11008])
    din("w3qT", [1024, 11008])
    din("w2qT", [2752, 4096])
    din("a1T", [4096, 2]); din("a3T", [4096, 2]); din("a2T", [11008, 2])
    din("b1T", [2, 11008]); din("b3T", [2, 11008]); din("b2T", [2, 4096])
    io["x_out"] = nc.dram_tensor("x_out", [D, T], F32, kind="ExternalOutput")
    with contextlib.ExitStack() as stk:
        em = L0FfnEmitter(nc, stk, io)
        em.debug = debug
        em.loads()
        em.ln_phase()
        if debug:
            em.dump_arena("d_xs", F_XS, 8192, bf=True)
            em.dump("d_xah", em.xah)
            em.dump("d_xah3", em.xah3)
        em.gateup_phase()
        if debug:
            em.dump_arena("d_gu", F_GU, 21504, bf=True)
            em.dump("d_guah", em.guah)
        em.down_phase()
        if debug:
            em.dump_arena("d_o2", F_O2, 16384, bf=True)
    return nc



def prep_l1attn(I):
    f32 = lambda a: np.ascontiguousarray(np.asarray(a, np.float32))
    sh = {}
    sh["Wo"] = f32(I["L1_attn_Wo"])
    sh["bo_c"] = _c128(I["L1_attn_bo"])
    masks = np.zeros((128, 4*512), np.float32)
    for dd in range(4):
        i = np.arange(128)[:, None]
        j = np.arange(512)[None, :]
        masks[:, dd*512:(dd+1)*512] = ((dd*128 + i) <= j).astype(np.float32)
    sh["maskc"] = np.ascontiguousarray(masks)
    sh["ident"] = np.eye(128, dtype=np.float32)
    return sh


def prep_l1attn_percore(I):
    pc = [dict() for _ in range(NCORE)]
    Wq = np.asarray(I["L1_attn_Wq"], np.float32)
    Wkv = np.asarray(I["L1_attn_Wkv"], np.float32)
    for c in range(NCORE):
        pc[c]["Wq_my"] = np.ascontiguousarray(Wq[:, 512*c:512*(c+1)])
        gA, gB = c // 2, 4 + c // 2
        pc[c]["Wkv_my"] = np.ascontiguousarray(
            np.concatenate([Wkv[:, 256*gA:256*gA+256], Wkv[:, 256*gB:256*gB+256]], axis=1))
        pc[c]["kvbase"] = np.array([[(c % 2)*4*4096*512]], dtype=np.int32)
    return pc


A_XBF, A_XSQ = 0, 2048
A_LNX = 4096
A_QST = 0
A_LNT = 4096
A_WQ = 20480
A_WKV = 0
A_LNT2 = 16384
A_KS = 36864
A_VS = 45056
A_VT = 20480
A_QHB = 28672
A_EXPT = 32768
A_CTXT = 34816
A_CTXG = 0
A_WOT = 16384
A_OBF = 32768
A_BSC1, A_BSC2 = 49152, 51200
AA_COLS = 53248


class L1AttnEmitter(LNMixin):
    def __init__(self, nc, stk, io, sems=None, prefix=""):
        self.nc = nc
        self.io = io
        sb = lambda n, sh, dt=F32: stk.enter_context(nc.sbuf_tensor(prefix + n, sh, dt))
        self.onesb = sb("onesb", [128, 1], BF16)
        self.onesrf = sb("onesrf", [1, 128])
        self.rows = sb("rows", [1, 2048])
        self.sbM = sb("sbM", [128, 512])
        self.sbR = sb("sbR", [128, 512])
        self.sbMb = sb("sbMb", [128, 512], BF16)
        self.sbRb = sb("sbRb", [128, 512], BF16)
        self.boS = sb("boS", [128, 32])
        self.identb = sb("identb", [128, 128], BF16)
        self.maskS = sb("maskS", [128, 2048], BF16)
        self.nshift = sb("nshift", [128, 1])
        self.kvbS = stk.enter_context(nc.sbuf_tensor(prefix + "kvbS", [1, 1], mybir.dt.int32))
        self.aF = sb("arenaF", [128, 16384])
        self.aB = sb("arenaB", [128, AA_COLS], BF16)
        psf = lambda n: stk.enter_context(nc.psum_tensor(prefix + n, [128, 512], F32))
        self.psA = psf("psA"); self.psB = psf("psB")
        self.psC2 = stk.enter_context(nc.psum_tensor(prefix + "psC2", [128, 1024], F32))
        self.psD2 = stk.enter_context(nc.psum_tensor(prefix + "psD2", [128, 1024], F32))
        self.psS1 = stk.enter_context(nc.psum_tensor(prefix + "psS1", [1, 512], F32))
        self._stk = stk
        self._prefix = prefix
        self.S = sems if sems is not None else Sems(nc, stk, SEM_NAMES)

    def loads(self):
        nc, S, io = self.nc, self.S, self.io
        with nc.Block() as blk:
            @blk.gpsimd
            def _(e):
                pl = Eng(e, S, "pl")
                pl.dma(self.boS[:, :], io["bo_c"][:, :], key="dma_pl")
                pl.dma(self.identb[:, :], io["ident"][:, :], key="dma_pl")
                pl.dma(self.maskS[:, :], io["maskc"][:, :], key="dma_pl")
                e.wait_ge(S.h["dma_pl"], S.n["dma_pl"])
            @blk.vector
            def _(e):
                v = Eng(e, S, "dv")
                v.op("memset", self.onesb[:, :], 1.0)
                v.op("memset", self.onesrf[:, :], 1.0)
                v.op("memset", self.nshift[:, :], -10.0)
        self.m_prm = S.n["dma_pl"]

    def ln_phase(self):
        nc, S, io, aF, aB = self.nc, self.S, self.io, self.aF, self.aB
        for g in range(2):
            with nc.Block() as blk:
                @blk.sync
                def _(e):
                    sy = Eng(e, S, "sy")
                    sy.dma(colap2(aF, 0, [[512, KC], [1, 512]]),
                           dramap(io["x_in"], g*512, [[T, 128], [128*T, KC], [1, 512]]),
                           key="dma_sy")
                m_xall = S.n["dma_sy"]

                m_mm0 = S.n["mm"]
                m_dv0 = S.n["dv"]

                @blk.tensor
                def _(e):
                    t = Eng(e, S, "mm")
                    for w in range(8):
                        t.wait("dv", m_dv0 + 8*(w + 1))
                        for kk in range(4):
                            k = 4*w + kk
                            e.matmul(self.psS1[:, :], self.onesb[:, :],
                                     colap(aB, A_XBF + kk*512, 512),
                                     start=(k == 0), stop=(k == KC - 1))
                            ins = e.matmul(self.psD2[0:1, 0:512], self.onesb[:, :],
                                           colap(aB, A_XSQ + kk*512, 512),
                                           start=(k == 0), stop=(k == KC - 1))
                        S.inc(ins, "mm", 1)

                @blk.vector
                def _(e):
                    v = Eng(e, S, "dv")
                    v.wait("dma_sy", m_xall)
                    for w in range(8):
                        if w >= 1:
                            v.wait("mm", m_mm0 + w)
                        for kk in range(4):
                            k = 4*w + kk
                            v.op("tensor_copy", colap(aB, A_XBF + kk*512, 512),
                                 colap(aF, 512*k, 512))
                            v.op("tensor_mul", colap(aB, A_XSQ + kk*512, 512),
                                 colap(aB, A_XBF + kk*512, 512),
                                 colap(aB, A_XBF + kk*512, 512))

                m_stats = S.n["mm"]
                m_rstd = self.ln_rows2(blk, 4096.0, m_stats, self.psS1, self.psD2)
                m_bc = self.bcast_mr(blk, m_rstd, self.psA, self.psB)

                @blk.vector
                def _(e):
                    v = Eng(e, S, "dv")
                    v.wait("dv", m_bc)
                    for k in range(KC):
                        v.op("tensor_sub", colap(aB, A_XBF + (k % 4)*512, 512),
                             colap(aF, 512*k, 512), self.sbMb[:, :])
                        v.op("tensor_mul", colap(aB, A_LNX + k*512, 512),
                             colap(aB, A_XBF + (k % 4)*512, 512), self.sbRb[:, :])
                m_lnx = S.n["dv"]

                @blk.sync
                def _(e):
                    sy = Eng(e, S, "sy")
                    e.wait_ge(S.h["dv"], m_lnx)
                    sy.dma(dramap(io["lnag_in%d" % g], 0, [[T2, 128], [128*T2, KC], [1, 512]]),
                           colap2(aB, A_LNX, [[512, KC], [1, 512]]), key="dma_sy")
                    e.wait_ge(S.h["dma_sy"], S.n["dma_sy"])

                @blk.gpsimd
                def _(e):
                    pl = Eng(e, S, "pl")
                    e.wait_ge(S.h["dma_sy"], S.n["dma_sy"])
                    ins = e.collective_compute(
                        "AllGather", ALU.bypass, replica_groups=[list(range(NCORE))],
                        ins=[io["lnag_in%d" % g].ap().opt()],
                        outs=[io["lnag_out%d" % g].ap().opt()])
                    S.inc(ins, "cc", 1)
                    if g == 0:
                        pl.dma(colap2(self.aB, A_WQ, [[512, KC], [1, 512]]),
                               dramap(self.io["Wq_my"], 0,
                                      [[512, 128], [128*512, KC], [1, 512]]),
                               key="dma_pl")
        self.m_cc_g = [S.n["cc"] - 1, S.n["cc"]]
        with nc.Block() as blk:
            @blk.gpsimd
            def _(e):
                e.wait_ge(S.h["dma_pl"], S.n["dma_pl"])
        self.m_wqkv = S.n["dma_pl"]

    def ln_rows2(self, blk, dim, mark_mm, ps1, ps2):
        S, rows = self.S, self.rows

        @blk.vector
        def _(e):
            v = Eng(e, S, "dv")
            v.wait("mm", mark_mm)
            v.op("tensor_scalar", rows[:, 0:512], ps1[0:1, :], 1.0/dim, None, ALU.mult)
            v.op("tensor_scalar", rows[:, 1024:1536], ps2[0:1, 0:512], 1.0/dim, EPS,
                 ALU.mult, ALU.add)
            v.op("tensor_mul", rows[:, 1536:2048], rows[:, 0:512], rows[:, 0:512])
            v.op("tensor_sub", rows[:, 1024:1536], rows[:, 1024:1536], rows[:, 1536:2048])
        m_var = S.n["dv"]

        @blk.scalar
        def _(e):
            sc = Eng(e, S, "ac")
            sc.wait("dv", m_var)
            sc.op("activation", rows[:, 1536:2048], rows[:, 1024:1536], AF.Ln)
            sc.op("activation", rows[:, 512:1024], rows[:, 1536:2048], AF.Exp,
                  scale=-0.5)
        m_sq = S.n["ac"]

        @blk.vector
        def _(e):
            v = Eng(e, S, "dv")
            v.wait("ac", m_sq)
        return S.n["dv"]

    def qkv_phase(self):
        nc, S, io, aB = self.nc, self.S, self.io, self.aB
        with nc.Block() as blk:
            m_mm0 = S.n["mm"]
            m_dv0 = S.n["dv"]
            q1b = [S.n["dma_qo"], S.n["dma_qo1"]]
            lx = [S.n["dma_lx"], S.n["dma_q0"]]

            @blk.sync
            def _(e):
                sy = Eng(e, S, "sy")
                for s in range(32):
                    tg, hh = s // 2, s % 2
                    if s >= 2:
                        e.wait_ge(S.h["mm"], m_mm0 + s - 1)
                    r = tg // 2
                    gq = tg % 2
                    e.wait_ge(S.h["cc"], self.m_cc_g[gq])
                    ins = e.dma_start(
                        out=colap2(aB, A_LNT + hh*8192, [[512, 16], [1, 512]]),
                        in_=dramap(io["lnag_out%d" % gq],
                                   r*4096*512 + hh*16*128*512,
                                   [[512, 128], [128*512, 16], [1, 512]]))
                    S.inc(ins, ["dma_lx", "dma_q0"][hh], 16)
            @blk.gpsimd
            def _(e):
                pl = Eng(e, S, "pl")
                for tg in range(16):
                    e.wait_ge(S.h["dv"], m_dv0 + 4*(tg + 1))
                    ins = e.dma_start(
                        out=dramap(io["qag"], tg*512,
                                   [[8192, 128], [8192*128, 4], [1, 512]]),
                        in_=colap2(aB, A_QST + (tg % 2)*2048, [[512, 4], [1, 512]]))
                    S.inc(ins, ["dma_qo", "dma_qo1"][tg % 2], 16)

            psq = [self.psC2[:, 0:512], self.psC2[:, 512:1024],
                   self.psD2[:, 0:512], self.psD2[:, 512:1024]]

            @blk.tensor
            def _(e):
                t = Eng(e, S, "mm")
                t.wait("dma_pl", self.m_wqkv)
                for s in range(32):
                    tg, hh = s // 2, s % 2
                    t.wait(["dma_lx", "dma_q0"][hh], lx[hh] + 16*(tg + 1))
                    if hh == 0 and tg >= 1:
                        t.wait("dv", m_dv0 + 4*tg)
                    for kk in range(16):
                        k = hh*16 + kk
                        ln = colap(aB, A_LNT + hh*8192 + kk*512, 512)
                        for fq in range(4):
                            ins = e.matmul(psq[fq],
                                           colap2(aB, A_WQ + k*512 + fq*128, [[1, 128]]),
                                           ln, start=(k == 0), stop=(k == KC - 1))
                    S.inc(ins, "mm", 1)

            @blk.vector
            def _(e):
                v = Eng(e, S, "dv")
                for tg in range(16):
                    v.wait("mm", m_mm0 + 2*tg + 2)
                    if tg >= 2:
                        v.wait(["dma_qo", "dma_qo1"][tg % 2],
                               q1b[tg % 2] + 16*(tg // 2))
                    for fq in range(4):
                        v.op("tensor_copy",
                             colap(aB, A_QST + (tg % 2)*2048 + fq*512, 512),
                             psq[fq], gate=(fq == 0))

        with nc.Block() as blk:
            @blk.gpsimd
            def _(e):
                pl = Eng(e, S, "pl")
                e.wait_ge(S.h["dma_qo"], S.n["dma_qo"])
                e.wait_ge(S.h["dma_qo1"], S.n["dma_qo1"])
                pl.dma(colap2(aB, A_WKV, [[512, KC], [1, 512]]),
                       dramap(io["Wkv_my"], 0, [[512, 128], [128*512, KC], [1, 512]]),
                       key="dma_pl")
                pl.dma(self.kvbS[:, :], io["kvbase"][:, :], key="dma_pl")
                e.wait_ge(S.h["dma_pl"], S.n["dma_pl"])
        self.m_wkv = S.n["dma_pl"]

        for tgl in range(8):
            with nc.Block() as blk:
                @blk.gpsimd
                def _(e):
                    pl = Eng(e, S, "pl")
                    e.wait_ge(S.h["dma_pl"], self.m_wkv)
                    with e.register("kvoff%d" % tgl) as roff:
                        e.reg_load(roff, self.kvbS[0:1, 0:1])
                        e.reg_add(roff, roff, (tgl // 2)*4096*512)
                        ins = e.dma_start(
                            out=colap2(aB, A_LNT2, [[512, KC], [1, 512]]),
                            in_=bass.AP(io["lnag_out%d" % (tgl % 2)].ap().tensor, roff,
                                        [[512, 128], [128*512, KC], [1, 512]]))
                        S.inc(ins, "dma_w1", 16)
                m_l1 = S.n["dma_w1"]

                psq = [self.psC2[:, 0:512], self.psC2[:, 512:1024],
                       self.psD2[:, 0:512], self.psD2[:, 512:1024]]

                @blk.tensor
                def _(e):
                    t = Eng(e, S, "mm")
                    t.wait("dma_w1", m_l1)
                    for k in range(KC):
                        ln = colap(aB, A_LNT2 + k*512, 512)
                        for i4 in range(4):
                            ins = e.matmul(psq[i4],
                                           colap2(aB, A_WKV + k*512 + i4*128, [[1, 128]]),
                                           ln, start=(k == 0), stop=(k == KC - 1))
                    S.inc(ins, "mm", 1)
                m_mmk = S.n["mm"]

                @blk.vector
                def _(e):
                    v = Eng(e, S, "dv")
                    v.wait("mm", m_mmk)
                    v.op("tensor_copy", colap(aB, A_KS + tgl*512, 512), psq[0])
                    v.op("tensor_copy", colap(aB, A_VS + tgl*512, 512), psq[1])
                    v.op("tensor_copy", colap(aB, A_KS + 4096 + tgl*512, 512), psq[2])
                    v.op("tensor_copy", colap(aB, A_VS + 4096 + tgl*512, 512), psq[3])

    def vt_phase(self):
        nc, S, aB = self.nc, self.S, self.aB
        with contextlib.ExitStack() as vstk:
            self.psTb = vstk.enter_context(
                nc.psum_tensor(self._prefix + "psTb", [128, 1024], BF16))
            self._vt_inner()

    def _vt_inner(self):
        nc, S, aB = self.nc, self.S, self.aB
        for grp in range(8):
            with nc.Block() as blk:
                @blk.tensor
                def _(e):
                    t = Eng(e, S, "mm")
                    t.wait("dma_pl", self.m_prm)
                    for j in range(8):
                        kt = grp*8 + j
                        ins = e.transpose(self.psTb[:, j*128:(j+1)*128],
                                          colap(aB, A_VS + kt*128, 128),
                                          self.identb[:, :])
                    S.inc(ins, "mm", 1)
                m_tp = S.n["mm"]

                @blk.vector
                def _(e):
                    v = Eng(e, S, "dv")
                    v.wait("mm", m_tp)
                    v.op("tensor_copy", colap(aB, A_VT + grp*1024, 1024), self.psTb[:, :])

    def flash_phase(self):
        nc, S, io, aB = self.nc, self.S, self.io, self.aB
        with contextlib.ExitStack() as fstk:
            psActx = [self.psA,
                      fstk.enter_context(nc.psum_tensor(self._prefix + "psA2",
                                                        [128, 512], F32))]
            self._flash_inner(psActx)

    def _flash_inner(self, psActx):
        nc, S, io, aB = self.nc, self.S, self.io, self.aB
        SCL = 1.0 / (128.0 ** 0.5)
        for u in range(8):
            h, b = u // 2, u % 2
            with nc.Block() as blk:
                @blk.sync
                def _(e):
                    sy = Eng(e, S, "sy")
                    sy.dma(colap(aB, A_QHB, 4096),
                           dramap(io["qag"], h*128*8192 + b*4096, [[8192, 128], [1, 4096]]),
                           key="dma_sy")
                m_q = S.n["dma_sy"]

                if u == 0:
                    self._ctx_free = [0, 0]
                    self._den_free = 0
                for qt in range(8):
                    nkt = 4*(qt + 1)
                    npair = nkt // 2
                    diag0 = nkt - 4
                    cslot = ((u*8 + qt) % 2)
                    m_mm0 = S.n["mm"]
                    m_ac0 = S.n["ac"]
                    m_dv0 = S.n["dv"]
                    mm_s = lambda p: m_mm0 + (1 if p == 0 else 2*p)
                    mm_c = lambda p: m_mm0 + (3 if p == 0 else 2*p + 3)
                    ac_e = lambda p: m_ac0 + p + 1
                    ndv_mask = {}
                    nm = 0
                    for jj in range(diag0, nkt):
                        nm += 1
                        ndv_mask[jj] = m_dv0 + nm
                    psPair = [self.psC2, self.psD2]

                    @blk.tensor
                    def _(e):
                        t = Eng(e, S, "mm")
                        if qt == 0:
                            t.wait("dma_sy", m_q)
                        t.wait("dv", max(self._ctx_free[cslot], self._den_free))

                        def consume(p):
                            t.wait("ac", ac_e(p))
                            for half in range(2):
                                jj = 2*p + half
                                if jj >= diag0:
                                    t.wait("dv", ndv_mask[jj])
                            for half in range(2):
                                jj = 2*p + half
                                expt = colap(aB, A_EXPT + (p % 2)*1024 + half*512, 512)
                                e.matmul(self.psS1[:, :], self.onesb[:, :], expt,
                                         start=(jj == 0), stop=(jj == nkt - 1))
                                ins = e.matmul(psActx[cslot][:, :],
                                               colap(aB, A_VT + b*4096 + jj*128, 128),
                                               expt, start=(jj == 0), stop=(jj == nkt - 1))
                            S.inc(ins, "mm", 1)

                        for p in range(npair):
                            if p >= 2:
                                t.wait("ac", ac_e(p - 2))
                            for half in range(2):
                                jj = 2*p + half
                                ins = e.matmul(psPair[p % 2][:, half*512:(half+1)*512],
                                               colap(aB, A_KS + b*4096 + jj*128, 128),
                                               colap(aB, A_QHB + qt*512, 512),
                                               start=True, stop=True)
                            S.inc(ins, "mm", 1)
                            if p >= 1:
                                consume(p - 1)
                        consume(npair - 1)

                    @blk.scalar
                    def _(e):
                        sc = Eng(e, S, "ac")
                        for p in range(npair):
                            sc.wait("mm", mm_s(p))
                            if p >= 2:
                                sc.wait("mm", mm_c(p - 2))
                            sc.op("activation", colap(aB, A_EXPT + (p % 2)*1024, 1024),
                                  psPair[p % 2][:, :], AF.Exp, scale=SCL,
                                  bias=self.nshift[:, :])

                    @blk.vector
                    def _(e):
                        v = Eng(e, S, "dv")
                        for jj in range(diag0, nkt):
                            p = jj // 2
                            v.wait("ac", ac_e(p))
                            v.op("tensor_mul",
                                 colap(aB, A_EXPT + (p % 2)*1024 + (jj % 2)*512, 512),
                                 colap(aB, A_EXPT + (p % 2)*1024 + (jj % 2)*512, 512),
                                 self.maskS[:, (jj - diag0)*512:(jj - diag0 + 1)*512])

                    m_done = S.n["mm"]
                    slot = (u*8 + qt) % 2
                    qkey = "dma_q%d" % slot

                    @blk.vector
                    def _(e):
                        v = Eng(e, S, "dv")
                        v.wait("mm", m_done)
                        v.op("tensor_copy", self.rows[:, 0:512], self.psS1[0:1, :])
                        self._den_free = S.n["dv"]
                    m_den = S.n["dv"]

                    @blk.scalar
                    def _(e):
                        sc = Eng(e, S, "ac")
                        sc.wait("dv", m_den)
                        sc.op("activation", self.rows[:, 1024:1536], self.rows[:, 0:512], AF.Ln)
                        sc.op("activation", self.rows[:, 512:1024], self.rows[:, 1024:1536],
                              AF.Exp, scale=-1.0)
                    m_dinv = S.n["ac"]

                    @blk.tensor
                    def _(e):
                        t = Eng(e, S, "mm")
                        t.wait("ac", m_dinv)
                        ins = e.matmul(self.psB[:, :], self.onesrf[:, :],
                                       self.rows[:, 512:1024], start=True, stop=True)
                        S.inc(ins, "mm", 1)
                    m_bc = S.n["mm"]

                    @blk.vector
                    def _(e):
                        v = Eng(e, S, "dv")
                        v.wait("mm", m_bc)
                        v.wait(qkey, S.n[qkey])
                        v.op("tensor_copy", self.sbR[:, :], self.psB[:, :])
                        v.op("tensor_mul", colap(aB, A_CTXT + slot*512, 512),
                             psActx[cslot][:, :], self.sbR[:, :])
                    m_ctx = S.n["dv"]
                    self._ctx_free[cslot] = m_ctx

                    @blk.sync
                    def _(e):
                        sy = Eng(e, S, "sy")
                        e.wait_ge(S.h["dv"], m_ctx)
                        gt0 = b*4096 + qt*512
                        jcore = gt0 // 1024
                        tcol = gt0 % 1024
                        ins = e.dma_start(
                            out=dramap(io["ctxag_in"], (512*jcore + 128*h)*1024 + tcol,
                                       [[1024, 128], [1, 512]]),
                            in_=colap(aB, A_CTXT + slot*512, 512))
                        S.inc(ins, qkey, 16)

        with nc.Block() as blk:
            @blk.gpsimd
            def _(e):
                pl = Eng(e, S, "pl")
                e.wait_ge(S.h["dma_q0"], S.n["dma_q0"])
                e.wait_ge(S.h["dma_q1"], S.n["dma_q1"])
                ins = e.collective_compute(
                    "AllToAll", ALU.bypass, replica_groups=[list(range(NCORE))],
                    ins=[io["ctxag_in"].ap().opt()],
                    outs=[io["ctxag_out"].ap().opt()])
                S.inc(ins, "cc", 1)
                e.wait_ge(S.h["cc"], S.n["cc"])

    def outproj_phase(self):
        nc, S, io, aF, aB = self.nc, self.S, self.io, self.aF, self.aB
        for g in range(2):
            with nc.Block() as blk:
                @blk.sync
                def _(e):
                    sy = Eng(e, S, "sy")
                    if g == 1:
                        e.wait_ge(S.h["dma_sy"], S.n["dma_sy"])
                    sy.dma(colap2(aF, 0, [[512, KC], [1, 512]]),
                           dramap(io["x_in"], g*512, [[T, 128], [128*T, KC], [1, 512]]),
                           key="dma_sy")
                    sy.dma(colap2(aB, A_CTXG, [[512, KC], [1, 512]]),
                           dramap(io["ctxag_out"], g*512, [[1024, 128], [128*1024, KC], [1, 512]]),
                           key="dma_sy")
                m_xall = S.n["dma_sy"]

                m_mm0 = S.n["mm"]
                m_ac0 = S.n["ac"]
                wb = [S.n["dma_o%d" % i] for i in range(4)]

                @blk.gpsimd
                def _(e):
                    pl = Eng(e, S, "pl")
                    for f in range(32):
                        if f >= 4:
                            e.wait_ge(S.h["mm"], m_mm0 + f - 3)
                        ins = e.dma_start(
                            out=colap2(aB, A_WOT + (f % 4)*4096, [[128, 32], [1, 128]]),
                            in_=dramap(io["Wo"], 128*f, [[4096, 128], [4096*128, 32], [1, 128]]))
                        S.inc(ins, "dma_o%d" % (f % 4), 16)

                @blk.tensor
                def _(e):
                    t = Eng(e, S, "mm")
                    for f in range(32):
                        pp = self.psA if (f % 2 == 0) else self.psB
                        t.wait("dma_o%d" % (f % 4), wb[f % 4] + 16*(f // 4 + 1))
                        t.wait("dma_sy", m_xall)
                        if f >= 2:
                            t.wait("ac", m_ac0 + (f - 1))
                        for cd in range(KC):
                            ins = e.matmul(pp[:, :],
                                           colap2(aB, A_WOT + (f % 4)*4096 + cd*128, [[1, 128]]),
                                           colap(aB, A_CTXG + cd*512, 512),
                                           start=(cd == 0), stop=(cd == KC - 1))
                        S.inc(ins, "mm", 1)

                @blk.scalar
                def _(e):
                    sc = Eng(e, S, "ac")
                    sc.wait("dma_pl", self.m_prm)
                    for f in range(32):
                        sc.wait("mm", m_mm0 + f + 1)
                        pp = self.psA if (f % 2 == 0) else self.psB
                        sc.op("activation", colap(aB, A_OBF + f*512, 512), pp[:, :],
                              AF.Identity, bias=self.boS[:, f:f+1], scale=1.0)
                m_oev0 = S.n["ac"] - 32

                m_dv0 = S.n["dv"]
                m_mmS0 = S.n["mm"]

                @blk.vector
                def _(e):
                    v = Eng(e, S, "dv")
                    for w in range(8):
                        if w >= 1:
                            v.wait("mm", m_mmS0 + w)
                        for kk in range(4):
                            f = 4*w + kk
                            v.wait("ac", m_oev0 + f + 1)
                            v.op("tensor_mul", colap(aB, A_BSC1 + kk*512, 512),
                                 colap(aB, A_OBF + f*512, 512),
                                 colap(aB, A_OBF + f*512, 512), gate=False)

                @blk.tensor
                def _(e):
                    t = Eng(e, S, "mm")
                    for w in range(8):
                        t.wait("dv", m_dv0 + 4*(w + 1))
                        for kk in range(4):
                            f = 4*w + kk
                            e.matmul(self.psS1[:, :], self.onesb[:, :],
                                     colap(aB, A_OBF + f*512, 512),
                                     start=(f == 0), stop=(f == 31))
                            ins = e.matmul(self.psD2[0:1, 0:512], self.onesb[:, :],
                                           colap(aB, A_BSC1 + kk*512, 512),
                                           start=(f == 0), stop=(f == 31))
                        S.inc(ins, "mm", 1)

                m_st = S.n["mm"]
                m_rstd = self.ln_rows2(blk, 4096.0, m_st, self.psS1, self.psD2)
                m_bc = self.bcast_mr(blk, m_rstd, self.psC2[:, 0:512], self.psD2[:, 0:512])

                @blk.vector
                def _(e):
                    v = Eng(e, S, "dv")
                    v.wait("dv", m_bc)
                    v.wait("dma_sy", m_xall)
                    for k in range(KC):
                        v.op("tensor_sub", colap(aB, A_BSC1 + (k % 4)*512, 512),
                             colap(aB, A_OBF + k*512, 512), self.sbMb[:, :])
                        v.op("tensor_mul", colap(aB, A_BSC2 + (k % 4)*512, 512),
                             colap(aB, A_BSC1 + (k % 4)*512, 512), self.sbRb[:, :])
                        v.op("tensor_add", colap(aF, 512*k, 512),
                             colap(aF, 512*k, 512), colap(aB, A_BSC2 + (k % 4)*512, 512))
                m_xn0 = S.n["dv"] - 3*KC

                @blk.sync
                def _(e):
                    sy = Eng(e, S, "sy")
                    for k in range(KC):
                        e.wait_ge(S.h["dv"], m_xn0 + 3*(k + 1))
                        sy.dma(dramap(io["x_out"], k*128*T + g*512, [[T, 128], [1, 512]]),
                               colap(aF, 512*k, 512), key="dma_sy")
                    if g == 1:
                        e.wait_ge(S.h["dma_sy"], S.n["dma_sy"])


def build_l1attn():
    from concourse.aot_env import allow_ici_all_to_all
    nc = bass.Bass("TRN2", target_bir_lowering=False)
    io = {}
    def din(name, shape, dt=F32):
        io[name] = nc.dram_tensor(name, shape, dt, kind="ExternalInput")
        return io[name]
    din("x_in", [D, T])
    din("Wq_my", [4096, 512])
    din("Wkv_my", [4096, 512])
    din("kvbase", [1, 1], mybir.dt.int32)
    din("Wo", [4096, 4096])
    din("bo_c", [128, 32])
    din("maskc", [128, 2048])
    din("ident", [128, 128])
    io["x_out"] = nc.dram_tensor("x_out", [D, T], F32, kind="ExternalOutput")
    io["lnag_in0"] = nc.dram_tensor("lnag_in0", [4096, 512], BF16)
    io["lnag_in1"] = nc.dram_tensor("lnag_in1", [4096, 512], BF16)
    io["lnag_out0"] = nc.dram_tensor("lnag_out0", [32768, 512], BF16, addr_space="Shared")
    io["lnag_out1"] = nc.dram_tensor("lnag_out1", [32768, 512], BF16, addr_space="Shared")
    io["qag"] = nc.dram_tensor("qag", [512, 8192], BF16)
    io["ctxag_in"] = nc.dram_tensor("ctxag_in", [4096, 1024], BF16)
    io["ctxag_out"] = nc.dram_tensor("ctxag_out", [4096, 1024], BF16)
    with contextlib.ExitStack() as stk, allow_ici_all_to_all():
        em = L1AttnEmitter(nc, stk, io)
        em.loads()
        em.ln_phase()
        em.qkv_phase()
        em.vt_phase()
        em.flash_phase()
        em.outproj_phase()
    return nc


def prep_l1ffn(I):
    f32 = lambda a: np.ascontiguousarray(np.asarray(a, np.float32))
    sh = {}
    sh["W_orgT"] = f32(I["L1_ffn_W_org"].T)
    sh["ldT"] = f32(np.asarray(I["L1_ffn_lora_down"], np.float32)
                    .transpose(1, 0, 2).reshape(4096, 512))
    sh["luT"] = f32(np.asarray(I["L1_ffn_lora_up"], np.float32).reshape(512, 4096))
    sh["WqTf"] = f32(I["L1_ffn_Wq"].T)
    sh["keysT"] = f32(I["L1_ffn_lora_keys"].T)
    E8 = np.zeros((8, 512), np.float32)
    for f in range(4):
        for i in range(128):
            E8[(f*128 + i)//64, f*128 + i] = 1.0
    sh["E8"] = np.ascontiguousarray(E8)
    return sh


def prep_l1ffn_percore(I):
    pc = [dict() for _ in range(NCORE)]
    for c in range(NCORE):
        b = c // 4
        oh = np.zeros((128, 2), np.float32); oh[:, b] = 1.0
        pc[c]["ohx"] = np.ascontiguousarray(oh)
        ohc = np.zeros((2, 1), np.float32); ohc[b, 0] = 1.0
        pc[c]["ohcol"] = np.ascontiguousarray(ohc)
    return pc


N_LNX = 0
N_LD = 16384
N_WOS = 32768
N_O4 = 40960
N_H2S = 57344
N_SCR = 61440
NB_COLS = 65536


class L1FfnEmitter(LNMixin):
    def __init__(self, nc, stk, io, sems=None, prefix=""):
        self.nc = nc
        self.io = io
        sb = lambda n, sh, dt=F32: stk.enter_context(nc.sbuf_tensor(prefix + n, sh, dt))
        self.onesb = sb("onesb", [128, 1], BF16)
        self.onesrf = sb("onesrf", [1, 128])
        self.rows = sb("rows", [1, 2048])
        self.sbM = sb("sbM", [128, 512])
        self.sbR = sb("sbR", [128, 512])
        self.sbMb = sb("sbMb", [128, 512], BF16)
        self.sbRb = sb("sbRb", [128, 512], BF16)
        self.sums = sb("sums", [128, 32])
        self.mean2 = sb("mean2", [128, 64])
        self.meanb = sb("meanb", [128, 64], BF16)
        self.keysb = sb("keysb", [128, 8], BF16)
        self.coefTb = sb("coefTb", [8, 1], BF16)
        self.ohxS = sb("ohxS", [128, 2])
        self.ohcS = sb("ohcS", [2, 1])
        self.qtS = sb("qtS", [128, 2], BF16)
        self.scS = sb("scS", [2, 16])
        self.coefT = sb("coefT", [8, 1])
        self.cvec = sb("cvec", [128, 4])
        self.red = sb("red", [128, 1])
        self.aF = sb("arenaF", [128, 16384])
        self.aB = sb("arenaB", [128, NB_COLS], BF16)
        psf = lambda n: stk.enter_context(nc.psum_tensor(prefix + n, [128, 512], F32))
        self.psA = psf("psA"); self.psB = psf("psB")
        self.psC = psf("psC"); self.psD = psf("psD")
        self.psM = psf("psM"); self.psE = psf("psE")
        self.psS1 = stk.enter_context(nc.psum_tensor(prefix + "psS1", [1, 512], F32))
        self.psS2 = stk.enter_context(nc.psum_tensor(prefix + "psS2", [1, 512], F32))
        self.S = sems if sems is not None else Sems(nc, stk, SEM_NAMES)

    def loads(self):
        nc, S, io = self.nc, self.S, self.io
        with nc.Block() as blk:
            @blk.gpsimd
            def _(e):
                pl = Eng(e, S, "pl")
                pl.dma(self.keysb[:, :], io["keysT"][:, :], key="dma_pl")
                pl.dma(bass.AP(self.aB[:, :].tensor, N_WOS + 4096,
                               [[self.aB[:, :].ap[0][0], 8], [1, 512]]),
                       io["E8"][:, :], key="dma_pl")
                pl.dma(self.ohxS[:, :], io["ohx"][:, :], key="dma_pl")
                pl.dma(self.ohcS[:, :], io["ohcol"][:, :], key="dma_pl")
                e.wait_ge(S.h["dma_pl"], S.n["dma_pl"])
            @blk.vector
            def _(e):
                v = Eng(e, S, "dv")
                v.op("memset", self.onesb[:, :], 1.0)
                v.op("memset", self.onesrf[:, :], 1.0)
                v.op("memset", self.sums[:, :], 0.0)
        self.m_prm = S.n["dma_pl"]

    def ln_phase(self):
        nc, S, io, aF, aB = self.nc, self.S, self.io, self.aF, self.aB
        for g in range(2):
            with nc.Block() as blk:
                @blk.sync
                def _(e):
                    sy = Eng(e, S, "sy")
                    sy.dma(colap2(aF, 0, [[512, KC], [1, 512]]),
                           dramap(io["x_in"], g*512, [[T, 128], [128*T, KC], [1, 512]]),
                           key="dma_sy")
                m_xall = S.n["dma_sy"]

                m_mm0 = S.n["mm"]
                m_dv0 = S.n["dv"]

                @blk.tensor
                def _(e):
                    t = Eng(e, S, "mm")
                    for w in range(8):
                        t.wait("dv", m_dv0 + 8*(w + 1))
                        for kk in range(4):
                            k = 4*w + kk
                            e.matmul(self.psS1[:, :], self.onesb[:, :],
                                     colap(aB, N_SCR + kk*512, 512),
                                     start=(k == 0), stop=(k == KC - 1))
                            ins = e.matmul(self.psS2[:, :], self.onesb[:, :],
                                           colap(aB, N_SCR + 2048 + kk*512, 512),
                                           start=(k == 0), stop=(k == KC - 1))
                        S.inc(ins, "mm", 1)

                @blk.vector
                def _(e):
                    v = Eng(e, S, "dv")
                    v.wait("dma_sy", m_xall)
                    for w in range(8):
                        if w >= 1:
                            v.wait("mm", m_mm0 + w)
                        for kk in range(4):
                            k = 4*w + kk
                            v.op("tensor_copy", colap(aB, N_SCR + kk*512, 512),
                                 colap(aF, 512*k, 512))
                            v.op("tensor_mul", colap(aB, N_SCR + 2048 + kk*512, 512),
                                 colap(aB, N_SCR + kk*512, 512),
                                 colap(aB, N_SCR + kk*512, 512))

                m_stats = S.n["mm"]
                m_rstd = self.ln_rows(blk, 4096.0, m_stats)
                m_bc = self.bcast_mr(blk, m_rstd, self.psA, self.psB)

                @blk.vector
                def _(e):
                    v = Eng(e, S, "dv")
                    v.wait("dv", m_bc)
                    for k in range(KC):
                        v.op("tensor_sub", colap(aB, N_SCR + (k % 4)*512, 512),
                             colap(aF, 512*k, 512), self.sbMb[:, :])
                        v.op("tensor_mul", colap(aB, N_LNX + k*512, 512),
                             colap(aB, N_SCR + (k % 4)*512, 512), self.sbRb[:, :])
                        v.op("reduce_sum", self.red[:, 0:1],
                             colap(aB, N_LNX + k*512, 512), axis=mybir.AxisListType.X)
                        v.op("tensor_add", self.sums[:, k:k+1], self.sums[:, k:k+1],
                             self.red[:, 0:1])
                m_lnx = S.n["dv"]

                @blk.sync
                def _(e):
                    sy = Eng(e, S, "sy")
                    e.wait_ge(S.h["dv"], m_lnx)
                    sy.dma(dramap(io["lnx3d"], g*512, [[T, 128], [128*T, KC], [1, 512]]),
                           colap2(aB, N_LNX, [[512, KC], [1, 512]]), key="dma_sy")
                    e.wait_ge(S.h["dma_sy"], S.n["dma_sy"])

    def mean_phase(self):
        nc, S, io = self.nc, self.S, self.io
        with nc.Block() as blk:
            @blk.vector
            def _(e):
                v = Eng(e, S, "dv")
                v.wait("dma_pl", self.m_prm)
                for k in range(KC):
                    sb_ap = bass.AP(self.sums[:, :].tensor, k,
                                    [[self.sums[:, :].ap[0][0], 128], [0, 2]])
                    v.op("tensor_tensor", self.mean2[:, 2*k:2*k+2], sb_ap,
                         self.ohxS[:, :], ALU.mult)
            m_m2 = S.n["dv"]

            @blk.sync
            def _(e):
                sy = Eng(e, S, "sy")
                e.wait_ge(S.h["dv"], m_m2)
                sy.dma(dramap(io["mean_in"], 0, [[2, 128], [256, 32], [1, 2]]),
                       colap2(self.mean2, 0, [[2, 32], [1, 2]]), key="dma_sy")
            m_mi = S.n["dma_sy"]

            @blk.gpsimd
            def _(e):
                pl = Eng(e, S, "pl")
                e.wait_ge(S.h["dma_sy"], m_mi)
                ins = e.collective_compute(
                    "AllReduce", ALU.add, replica_groups=[list(range(NCORE))],
                    ins=[io["mean_in"].ap().opt()], outs=[io["mean_out"].ap().opt()])
                S.inc(ins, "cc", 1)
                e.wait_ge(S.h["cc"], S.n["cc"])
                pl.dma(colap2(self.mean2, 0, [[2, 32], [1, 2]]),
                       dramap(io["mean_out"], 0, [[2, 128], [256, 32], [1, 2]]),
                       key="dma_pl")
            m_mo = S.n["dma_pl"]

            @blk.vector
            def _(e):
                v = Eng(e, S, "dv")
                v.wait("dma_pl", m_mo)
                v.op("tensor_copy", self.meanb[:, :], self.mean2[:, :])

        with nc.Block() as blk:
            @blk.gpsimd
            def _(e):
                pl = Eng(e, S, "pl")
                pl.dma(colap(self.aB, N_WOS, 4096),
                       dramap(self.io["WqTf"], 0, [[128, 128], [128*128, KC], [1, 128]]),
                       key="dma_pl")
            m_wq = S.n["dma_pl"]

            @blk.tensor
            def _(e):
                t = Eng(e, S, "mm")
                t.wait("dma_pl", m_wq)
                t.wait("dv", S.n["dv"])
                for k in range(KC):
                    ins = e.matmul(self.psA[:, 0:2],
                                   colap(self.aB, N_WOS + k*128, 128),
                                   self.meanb[:, 2*k:2*k+2],
                                   start=(k == 0), stop=(k == KC - 1))
                S.inc(ins, "mm", 1)
            m_q = S.n["mm"]

            @blk.vector
            def _(e):
                v = Eng(e, S, "dv")
                v.wait("mm", m_q)
                v.op("tensor_copy", self.qtS[:, :], self.psA[:, 0:2])
            m_qt = S.n["dv"]

            @blk.tensor
            def _(e):
                t = Eng(e, S, "mm")
                t.wait("dv", m_qt)
                ins = e.matmul(self.psB[0:2, 0:8], self.qtS[:, :], self.keysb[:, :],
                               start=True, stop=True)
                S.inc(ins, "mm", 1)
            m_sc = S.n["mm"]

            @blk.vector
            def _(e):
                v = Eng(e, S, "dv")
                v.wait("mm", m_sc)
                SCL2 = 1.0 / (4096.0 * (128.0 ** 0.5))
                v.op("tensor_scalar", self.scS[0:2, 0:8], self.psB[0:2, 0:8],
                     SCL2, None, ALU.mult)
                v.op("reduce_max", self.scS[0:2, 8:9], self.scS[0:2, 0:8],
                     axis=mybir.AxisListType.X)
                v.op("tensor_scalar", self.scS[0:2, 0:8], self.scS[0:2, 0:8],
                     self.scS[0:2, 8:9], None, ALU.subtract)
            m_sub = S.n["dv"]

            @blk.scalar
            def _(e):
                sc = Eng(e, S, "ac")
                sc.wait("dv", m_sub)
                sc.op("activation", self.scS[0:2, 0:8], self.scS[0:2, 0:8], AF.Exp)
            m_exp = S.n["ac"]

            @blk.vector
            def _(e):
                v = Eng(e, S, "dv")
                v.wait("ac", m_exp)
                v.op("reduce_sum", self.scS[0:2, 8:9], self.scS[0:2, 0:8],
                     axis=mybir.AxisListType.X)
                v.op("reciprocal", self.scS[0:2, 9:10], self.scS[0:2, 8:9])
                v.op("tensor_scalar", self.scS[0:2, 0:8], self.scS[0:2, 0:8],
                     self.scS[0:2, 9:10], None, ALU.mult)
            m_cf = S.n["dv"]

            @blk.tensor
            def _(e):
                t = Eng(e, S, "mm")
                t.wait("dv", m_cf)
                ins = e.matmul(self.psC[0:8, 0:1], self.scS[0:2, 0:8], self.ohcS[:, :],
                               start=True, stop=True)
                S.inc(ins, "mm", 1)
            m_ct = S.n["mm"]

            @blk.vector
            def _(e):
                v = Eng(e, S, "dv")
                v.wait("mm", m_ct)
                v.op("tensor_copy", self.coefT[:, :], self.psC[0:8, 0:1])
                v.op("tensor_copy", self.coefTb[:, :], self.coefT[:, :])
            m_ct2 = S.n["dv"]

            @blk.tensor
            def _(e):
                t = Eng(e, S, "mm")
                t.wait("dv", m_ct2)
                for f in range(4):
                    e8 = bass.AP(self.aB[:, :].tensor, N_WOS + 4096 + f*128,
                                 [[self.aB[:, :].ap[0][0], 8], [1, 128]])
                    ins = e.matmul(self.psD[:, f:f+1], e8,
                                   self.coefTb[:, :], start=True, stop=True)
                S.inc(ins, "mm", 1)
            m_cv = S.n["mm"]

            @blk.vector
            def _(e):
                v = Eng(e, S, "dv")
                v.wait("mm", m_cv)
                v.op("tensor_copy", self.cvec[:, :], self.psD[:, 0:4])

    def delta_phase(self):
        nc, S, io, aB = self.nc, self.S, self.io, self.aB
        with nc.Block() as blk:
            @blk.gpsimd
            def _(e):
                pl = Eng(e, S, "pl")
                pl.dma(colap2(aB, N_LD, [[512, KC], [1, 512]]),
                       dramap(io["ldT"], 0, [[512, 128], [128*512, KC], [1, 512]]),
                       key="dma_pl")
        m_ld = S.n["dma_pl"]
        for g in range(2):
            with nc.Block() as blk:
                @blk.sync
                def _(e):
                    sy = Eng(e, S, "sy")
                    sy.dma(colap2(aB, N_LNX, [[512, KC], [1, 512]]),
                           dramap(io["lnx3d"], g*512, [[T, 128], [128*T, KC], [1, 512]]),
                           key="dma_sy")
                m_ln = S.n["dma_sy"]

                m_mm0 = S.n["mm"]

                @blk.tensor
                def _(e):
                    t = Eng(e, S, "mm")
                    t.wait("dma_pl", m_ld)
                    t.wait("dma_sy", m_ln)
                    t.wait("dv", S.n["dv"])
                    psq = [self.psA, self.psB, self.psC, self.psD]
                    for f in range(4):
                        for k in range(KC):
                            ins = e.matmul(psq[f][:, :],
                                           colap2(aB, N_LD + k*512 + f*128, [[1, 128]]),
                                           colap(aB, N_LNX + k*512, 512),
                                           start=(k == 0), stop=(k == KC - 1))
                        S.inc(ins, "mm", 1)

                @blk.vector
                def _(e):
                    v = Eng(e, S, "dv")
                    psq = [self.psA, self.psB, self.psC, self.psD]
                    for f in range(4):
                        v.wait("mm", m_mm0 + f + 1)
                        v.op("tensor_scalar", colap(aB, N_H2S + f*1024 + g*512, 512),
                             psq[f][:, :], self.cvec[:, f:f+1], None, ALU.mult)

    def base_phase(self):
        nc, S, io, aB, aF = self.nc, self.S, self.io, self.aB, self.aF
        with nc.Block() as blk:
            @blk.gpsimd
            def _(e):
                pl = Eng(e, S, "pl")
                pl.dma(colap2(aB, N_LD, [[4096, 4], [1, 4096]]),
                       dramap(io["luT"], 0, [[4096, 128], [128*4096, 4], [1, 4096]]),
                       key="dma_pl")
        m_lu = S.n["dma_pl"]
        for g in range(2):
            with nc.Block() as blk:
                @blk.sync
                def _(e):
                    sy = Eng(e, S, "sy")
                    sy.dma(colap2(aF, 0, [[512, KC], [1, 512]]),
                           dramap(io["x_in"], g*512, [[T, 128], [128*T, KC], [1, 512]]),
                           key="dma_sy")
                    sy.dma(colap2(aB, N_LNX, [[512, KC], [1, 512]]),
                           dramap(io["lnx3d"], g*512, [[T, 128], [128*T, KC], [1, 512]]),
                           key="dma_sy")
                m_xall = S.n["dma_sy"]

                m_mm0 = S.n["mm"]
                m_ac0 = S.n["ac"]
                wb = [S.n["dma_w0"], S.n["dma_w1"]]

                @blk.gpsimd
                def _(e):
                    pl = Eng(e, S, "pl")
                    for f in range(32):
                        slot = f % 2
                        if f >= 2:
                            e.wait_ge(S.h["mm"], m_mm0 + f - 1)
                        ins = e.dma_start(
                            out=colap2(aB, N_WOS + slot*4096, [[128, 32], [1, 128]]),
                            in_=dramap(io["W_orgT"], 128*f,
                                       [[4096, 128], [4096*128, 32], [1, 128]]))
                        S.inc(ins, "dma_w%d" % slot, 16)

                @blk.tensor
                def _(e):
                    t = Eng(e, S, "mm")
                    t.wait("dma_pl", m_lu)
                    for f in range(32):
                        slot = f % 2
                        pp = self.psA if (f % 2 == 0) else self.psB
                        t.wait("dma_w%d" % slot, wb[slot] + 16*(f // 2 + 1))
                        t.wait("dma_sy", m_xall)
                        if f >= 2:
                            t.wait("ac", m_ac0 + (f - 1))
                        for cd in range(KC):
                            e.matmul(pp[:, :],
                                     colap2(aB, N_WOS + slot*4096 + cd*128, [[1, 128]]),
                                     colap(aB, N_LNX + cd*512, 512),
                                     start=(cd == 0), stop=False)
                        for er in range(4):
                            ins = e.matmul(pp[:, :],
                                           colap2(aB, N_LD + er*4096 + f*128, [[1, 128]]),
                                           colap(aB, N_H2S + er*1024 + g*512, 512),
                                           start=False, stop=(er == 3))
                        S.inc(ins, "mm", 1)

                @blk.scalar
                def _(e):
                    sc = Eng(e, S, "ac")
                    for f in range(32):
                        sc.wait("mm", m_mm0 + f + 1)
                        pp = self.psA if (f % 2 == 0) else self.psB
                        sc.op("activation", colap(aB, N_O4 + f*512, 512), pp[:, :],
                              AF.Identity)
                m_oev0 = S.n["ac"] - 32

                m_dv0 = S.n["dv"]
                m_mmS0 = S.n["mm"]

                @blk.vector
                def _(e):
                    v = Eng(e, S, "dv")
                    for w in range(8):
                        if w >= 1:
                            v.wait("mm", m_mmS0 + w)
                        for kk in range(4):
                            f = 4*w + kk
                            v.wait("ac", m_oev0 + f + 1)
                            v.op("tensor_mul", colap(aB, N_SCR + kk*512, 512),
                                 colap(aB, N_O4 + f*512, 512),
                                 colap(aB, N_O4 + f*512, 512), gate=False)

                @blk.tensor
                def _(e):
                    t = Eng(e, S, "mm")
                    for w in range(8):
                        t.wait("dv", m_dv0 + 4*(w + 1))
                        for kk in range(4):
                            f = 4*w + kk
                            e.matmul(self.psS1[:, :], self.onesb[:, :],
                                     colap(aB, N_O4 + f*512, 512),
                                     start=(f == 0), stop=(f == 31))
                            ins = e.matmul(self.psS2[:, :], self.onesb[:, :],
                                           colap(aB, N_SCR + kk*512, 512),
                                           start=(f == 0), stop=(f == 31))
                        S.inc(ins, "mm", 1)

                m_st = S.n["mm"]
                m_rstd = self.ln_rows(blk, 4096.0, m_st)
                m_bc = self.bcast_mr(blk, m_rstd, self.psC, self.psD)

                @blk.vector
                def _(e):
                    v = Eng(e, S, "dv")
                    v.wait("dv", m_bc)
                    v.wait("dma_sy", m_xall)
                    for k in range(KC):
                        v.op("tensor_sub", colap(aB, N_SCR + (k % 4)*512, 512),
                             colap(aB, N_O4 + k*512, 512), self.sbMb[:, :])
                        v.op("tensor_mul", colap(aB, N_SCR + 2048 + (k % 4)*512, 512),
                             colap(aB, N_SCR + (k % 4)*512, 512), self.sbRb[:, :])
                        v.op("tensor_add", colap(aF, 512*k, 512),
                             colap(aF, 512*k, 512), colap(aB, N_SCR + 2048 + (k % 4)*512, 512))
                m_xn0 = S.n["dv"] - 3*KC

                @blk.sync
                def _(e):
                    sy = Eng(e, S, "sy")
                    for k in range(KC):
                        e.wait_ge(S.h["dv"], m_xn0 + 3*(k + 1))
                        sy.dma(dramap(io["x_out"], k*128*T + g*512, [[T, 128], [1, 512]]),
                               colap(aF, 512*k, 512), key="dma_sy")
                    e.wait_ge(S.h["dma_sy"], S.n["dma_sy"])


def build_l1ffn():
    nc = bass.Bass("TRN2", target_bir_lowering=False)
    io = {}
    def din(name, shape, dt=F32):
        io[name] = nc.dram_tensor(name, shape, dt, kind="ExternalInput")
        return io[name]
    din("x_in", [D, T])
    din("W_orgT", [4096, 4096])
    din("ldT", [4096, 512])
    din("luT", [512, 4096])
    din("WqTf", [4096, 128])
    din("keysT", [128, 8])
    din("E8", [8, 512])
    din("ohx", [128, 2])
    din("ohcol", [2, 1])
    io["x_out"] = nc.dram_tensor("x_out", [D, T], F32, kind="ExternalOutput")
    io["lnx3d"] = nc.dram_tensor("lnx3d", [4096, 1024], BF16)
    io["mean_in"] = nc.dram_tensor("mean_in", [4096, 2], F32)
    io["mean_out"] = nc.dram_tensor("mean_out", [4096, 2], F32)
    with contextlib.ExitStack() as stk:
        em = L1FfnEmitter(nc, stk, io)
        em.loads()
        em.ln_phase()
        em.mean_phase()
        em.delta_phase()
        em.base_phase()
    return nc


def build_fused(debug=False):
    from concourse.aot_env import allow_ici_all_to_all
    nc = bass.Bass("TRN2", target_bir_lowering=False)
    io = {}
    def din(name, shape, dt=F32):
        io[name] = nc.dram_tensor(name, shape, dt, kind="ExternalInput")
        return io[name]
    din("x_in", [D, T])
    din("WinT", [D, 512]); din("prm_in", [128, 300]); din("selc", [128, 36])
    din("W1T", [4, 512, 2048]); din("W2T", [4, 2048, 512]); din("WoutT", [512, 4096])
    din("w1qT", [1024, 11008]); din("w3qT", [1024, 11008]); din("w2qT", [2752, 4096])
    din("a1T", [4096, 2]); din("a3T", [4096, 2]); din("a2T", [11008, 2])
    din("b1T", [2, 11008]); din("b3T", [2, 11008]); din("b2T", [2, 4096])
    din("Wq_my", [4096, 512]); din("Wkv_my", [4096, 512])
    din("kvbase", [1, 1], mybir.dt.int32)
    din("Wo", [4096, 4096]); din("bo_c", [128, 32]); din("maskc", [128, 2048])
    din("ident", [128, 128])
    din("W_orgT", [4096, 4096]); din("ldT", [4096, 512]); din("luT", [512, 4096])
    din("WqTf", [4096, 128]); din("keysT", [128, 8]); din("E8", [8, 512])
    din("ohx", [128, 2]); din("ohcol", [2, 1])
    x_out = nc.dram_tensor("x_out", [D, T], F32, kind="ExternalOutput")
    x1d = nc.dram_tensor("x1d", [D, T], F32)
    x2d = nc.dram_tensor("x2d", [D, T], F32)
    x3d = nc.dram_tensor("x3d", [D, T], F32)
    fag_in = [nc.dram_tensor(f"fag_in{l}", [1, 1024], F32) for l in range(4)]
    fag_out = [nc.dram_tensor(f"fag_out{l}", [8, 1024], F32, addr_space="Shared")
               for l in range(4)]
    lnag_in0 = nc.dram_tensor("lnag_in0", [4096, 512], BF16)
    lnag_in1 = nc.dram_tensor("lnag_in1", [4096, 512], BF16)
    lnag_out0 = nc.dram_tensor("lnag_out0", [32768, 512], BF16, addr_space="Shared")
    lnag_out1 = nc.dram_tensor("lnag_out1", [32768, 512], BF16, addr_space="Shared")
    qag = nc.dram_tensor("qag", [512, 8192], BF16)
    ctxag_in = nc.dram_tensor("ctxag_in", [4096, 1024], BF16)
    ctxag_out = nc.dram_tensor("ctxag_out", [4096, 1024], BF16)
    lnx3d = nc.dram_tensor("lnx3d", [4096, 1024], BF16)
    mean_in = nc.dram_tensor("mean_in", [4096, 2], F32)
    mean_out = nc.dram_tensor("mean_out", [4096, 2], F32, addr_space="Shared")

    with contextlib.ExitStack() as stk0, allow_ici_all_to_all():
        S = Sems(nc, stk0, SEM_NAMES)
        io0 = dict(io, x_out=x1d, fag_in=fag_in, fag_out=fag_out)
        with contextlib.ExitStack() as stk:
            em = L0AttnEmitter(nc, stk, io0, sems=S, prefix="p0_")
            em.debug = debug
            em.loads()
            em.inproj()
            em.prefetch_ffn(0)
            for l in range(4):
                em.inner_layer(l)
            em.final()
        io1 = dict(io, x_in=x1d, x_out=x2d)
        with contextlib.ExitStack() as stk:
            em = L0FfnEmitter(nc, stk, io1, sems=S, prefix="p1_")
            em.debug = debug
            em.loads()
            em.ln_phase()
            em.gateup_phase()
            em.down_phase()
        io2 = dict(io, x_in=x2d, x_out=x3d, lnag_in0=lnag_in0, lnag_in1=lnag_in1,
                   lnag_out0=lnag_out0, lnag_out1=lnag_out1,
                   qag=qag, ctxag_in=ctxag_in, ctxag_out=ctxag_out)
        with contextlib.ExitStack() as stk:
            em = L1AttnEmitter(nc, stk, io2, sems=S, prefix="p2_")
            em.debug = debug
            em.loads()
            em.ln_phase()
            em.qkv_phase()
            em.vt_phase()
            em.flash_phase()
            em.outproj_phase()
        io3 = dict(io, x_in=x3d, x_out=x_out, lnx3d=lnx3d,
                   mean_in=mean_in, mean_out=mean_out)
        with contextlib.ExitStack() as stk:
            em = L1FfnEmitter(nc, stk, io3, sems=S, prefix="p3_")
            em.debug = debug
            em.loads()
            em.ln_phase()
            em.mean_phase()
            em.delta_phase()
            em.base_phase()
    return nc


_NCS = {}


def _get_nc(name, builder):
    if name not in _NCS:
        _NCS[name] = builder()
    return _NCS[name]


def kernel(**inputs):
    sh = {}
    sh.update(prep_l0attn(inputs))
    sh.update(prep_l0ffn(inputs))
    sh.update(prep_l1attn(inputs))
    sh.update(prep_l1ffn(inputs))
    pc = prep_percore(inputs)
    pc1a = prep_l1attn_percore(inputs)
    pc1f = prep_l1ffn_percore(inputs)
    nc = _get_nc("fused", build_fused)
    in_maps = []
    for c in range(NCORE):
        m = dict(sh)
        m["x_in"] = pc[c]["x_in"]
        m["selc"] = pc[c]["selc"]
        m.update(pc1a[c])
        m.update(pc1f[c])
        in_maps.append(m)
    res = run_bass_kernel_spmd(nc, in_maps, core_ids=list(range(NCORE)))
    out = np.concatenate([np.asarray(r["x_out"]).T for r in res.results], axis=0)
    return np.ascontiguousarray(out.reshape(2, 4096, 4096))
